# version 04
# baseline (speedup 1.0000x reference)
import contextlib
import numpy as np
import ml_dtypes
import concourse.bass as bass
import concourse.mybir as mybir
from concourse.bass_utils import run_bass_kernel_spmd

F32 = mybir.dt.float32
BF16 = mybir.dt.bfloat16
AF = mybir.ActivationFunctionType
ALU = mybir.AluOpType
AX = mybir.AxisListType

MAGIC = 1.5 * 2 ** 23
EPS = 1e-5
D = 1024
NKV = 4096
NQ = 2048
H = 16
II = 2816
NT_KV = 32
NT_Q = 16
NTT = 48
DC = 8
JC = 22


def build_nc(phases=5, debug=(), skip=()):
    nc = bass.Bass("TRN2", target_bir_lowering=False)
    ap = lambda t: t.ap()

    x_kv = ap(nc.dram_tensor("x_kv", [NKV, D], F32, kind="ExternalInput"))
    x_q = ap(nc.dram_tensor("x_q", [NQ, D], F32, kind="ExternalInput"))
    Wq = ap(nc.dram_tensor("Wq", [D, D], F32, kind="ExternalInput"))
    Wk = ap(nc.dram_tensor("Wk", [D, D], F32, kind="ExternalInput"))
    Wv = ap(nc.dram_tensor("Wv", [D, D], F32, kind="ExternalInput"))
    Wo = ap(nc.dram_tensor("Wo", [D, D], F32, kind="ExternalInput"))
    wg_d = ap(nc.dram_tensor("wg", [II, D], F32, kind="ExternalInput"))
    wu_d = ap(nc.dram_tensor("wu", [II, D], F32, kind="ExternalInput"))
    wd_d = ap(nc.dram_tensor("wd", [D, II], F32, kind="ExternalInput"))
    vec_pc_d = ap(nc.dram_tensor("vec_pc", [128, 24], F32, kind="ExternalInput"))
    alpha_bc_d = ap(nc.dram_tensor("alpha_bc", [128, 16], F32, kind="ExternalInput"))
    bv_bc_d = ap(nc.dram_tensor("bv_bc", [128, D], F32, kind="ExternalInput"))
    gm_bc_d = ap(nc.dram_tensor("gm_bc", [128, D], F32, kind="ExternalInput"))
    bt_bc_d = ap(nc.dram_tensor("bt_bc", [128, D], F32, kind="ExternalInput"))
    clip2_d = ap(nc.dram_tensor("clip2", [1, 8], F32, kind="ExternalInput"))
    ident_d = ap(nc.dram_tensor("ident", [128, 128], F32, kind="ExternalInput"))
    ident16_d = ap(nc.dram_tensor("ident16", [128, 128], BF16, kind="ExternalInput"))
    ones_d = ap(nc.dram_tensor("ones_row", [1, 128], F32, kind="ExternalInput"))
    sel2_d = ap(nc.dram_tensor("sel2", [2, 128], F32, kind="ExternalInput"))
    out_d = ap(nc.dram_tensor("out", [NQ, D], F32, kind="ExternalOutput"))

    dbg = {}
    for name, shape, dt in [("xn_fm", [D, NKV], BF16), ("xq_fm", [D, NQ], BF16),
                            ("gscale", [1, 4], F32), ("kscr", [D, NKV], BF16),
                            ("qscr", [D, NQ], BF16), ("vscr", [NKV, H * 65], BF16),
                            ("crow", [1, 16], F32), ("biaspc", [128, 16], F32),
                            ("attn", [D, NQ], BF16), ("hscr", [II, NQ], BF16),
                            ("ffn", [D, NQ], BF16), ("xnqscr", [D, NQ], BF16)]:
        if name in debug:
            dbg[name] = ap(nc.dram_tensor("dbg_" + name, shape, dt, kind="ExternalOutput"))

    wq16s = ap(nc.dram_tensor("wq16s", [D, D], BF16))
    wk16s = ap(nc.dram_tensor("wk16s", [D, D], BF16))
    wv16s = ap(nc.dram_tensor("wv16s", [D, D], BF16))
    wo16s = ap(nc.dram_tensor("wo16s", [D, D], BF16))
    kscr = ap(nc.dram_tensor("kscr", [D, NKV], BF16))
    qscr = ap(nc.dram_tensor("qscr", [D, NQ], BF16))
    xnqscr = ap(nc.dram_tensor("xnqscr", [D, NQ], BF16))
    vscr = ap(nc.dram_tensor("vscr", [NKV, H * 65], BF16))
    kmaxscr = ap(nc.dram_tensor("kmaxscr", [DC, 128], F32))
    wgq_scr = ap(nc.dram_tensor("wgq_scr", [II, D], BF16))
    wuq_scr = ap(nc.dram_tensor("wuq_scr", [II, D], BF16))
    wdq_scr = ap(nc.dram_tensor("wdq_scr", [D, II], BF16))
    sgscr = ap(nc.dram_tensor("sgscr", [II, NQ], F32))
    upscr = ap(nc.dram_tensor("upscr", [II, NQ], F32))
    hscr = ap(nc.dram_tensor("hscr", [II, NQ], BF16))
    hqscr = ap(nc.dram_tensor("hqscr", [II, NQ], BF16))
    dscr = ap(nc.dram_tensor("dscr", [H, NQ], F32))
    ccin1 = ap(nc.dram_tensor("ccin1", [8], F32))
    ccout1 = ap(nc.dram_tensor("ccout1", [8], F32))
    ccin2 = ap(nc.dram_tensor("ccin2", [8], F32))
    ccout2 = ap(nc.dram_tensor("ccout2", [8], F32))

    xkv_t = x_kv.rearrange("(t p) d -> t p d", p=128)
    xq_t = x_q.rearrange("(t p) d -> t p d", p=128)
    out_t = out_d.rearrange("(t p) d -> t p d", p=128)

    es = contextlib.ExitStack()
    with es:
        sbg = lambda name, shape, dt=F32: es.enter_context(nc.sbuf_tensor(name, shape, dt))
        ident = sbg("identS", [128, 128])
        ident16 = sbg("ident16S", [128, 128], BF16)
        ones_row = sbg("ones_rowS", [1, 128])
        sel2 = sbg("sel2S", [2, 128])
        vpc = sbg("vpcS", [128, 24])
        alpha_bc = sbg("alpha_bcS", [128, 16])
        alpha_s = sbg("alpha_sS", [128, 16])
        bias_pc = sbg("bias_pcS", [128, 16])
        bv_bc = sbg("bv_bcS", [128, D])
        clip2 = sbg("clip2S", [1, 8])
        sig2 = sbg("sig2S", [1, 8])
        c_row = sbg("c_rowS", [1, 16])
        gacc = sbg("gaccS", [128, 1])
        hacc = sbg("haccS", [128, 1])
        gmax_g = sbg("gmax_gS", [1, 8])
        hmax_g = sbg("hmax_gS", [1, 8])
        cc8 = sbg("cc8S", [1, 8])
        sc4 = sbg("sc4S", [1, 4])
        isx_pc = sbg("isx_pcS", [128, 1])
        sx_pc = sbg("sx_pcS", [128, 1])
        is2_pc = sbg("is2_pcS", [128, 1])
        s2_pc = sbg("s2_pcS", [128, 1])
        swg = sbg("swgS", [128, JC])
        swu = sbg("swuS", [128, JC])
        swd = sbg("swdS", [128, DC])

        sem = lambda name: es.enter_context(nc.semaphore(name))
        smSET = sem("smSET"); smWC = sem("smWC")
        smSQ = sem("smSQ"); smVS = sem("smVS"); smRS = sem("smRS")
        smXR = sem("smXR"); smCA = sem("smCA"); smTR = sem("smTR"); smEV = sem("smEV")
        smGM = sem("smGM"); smCC = sem("smCC"); smNQ = sem("smNQ")
        smBC = sem("smBC"); smQV = sem("smQV"); smT2 = sem("smT2"); smE2 = sem("smE2")
        smWT = sem("smWT"); smXN2 = sem("smXN2"); smPM = sem("smPM"); smEB = sem("smEB")
        smKXk = sem("smKXk")
        smAB = sem("smAB"); smKM = sem("smKM"); smBM = sem("smBM")
        smBRa = sem("smBRa"); smBRb = sem("smBRb"); smTRb = sem("smTRb")
        smVL = sem("smVL"); smKL0 = sem("smKL0"); smKL1 = sem("smKL1")
        smSC = sem("smSC"); smEX = sem("smEX"); smAV = sem("smAV")
        smRC = sem("smRC"); smBB = sem("smBB"); smNM = sem("smNM"); smHD = sem("smHD")
        smWQ = sem("smWQ")
        smGM2 = sem("smGM2"); smGE = sem("smGE"); smSCG = sem("smSCG")
        smUM = sem("smUM"); smUE = sem("smUE"); smHP = sem("smHP")
        smHQ = sem("smHQ"); smDM = sem("smDM"); smDE = sem("smDE")
        smXQL = sem("smXQL"); smYM = sem("smYM"); smYE = sem("smYE")
        smYT = sem("smYT"); smYV = sem("smYV")
        smLN2 = sem("smLN2"); smFT = sem("smFT"); smFV = sem("smFV")
        smOUT = sem("smOUT"); smOUTv = sem("smOUTv"); smDBG = sem("smDBG")
        smXAp = [sem("smXAp0"), sem("smXAp1")]
        smSBp = [sem("smSBp0"), sem("smSBp1")]
        smSBvp = [sem("smSBvp0"), sem("smSBvp1")]
        smKXs = sem("smKXs")
        smQLp = [sem("smQLp0"), sem("smQLp1")]
        smWLp = [sem("smWLp0"), sem("smWLp1")]
        smWSp = [sem("smWSp0"), sem("smWSp1")]
        smSGSp = [sem("smSGSp0"), sem("smSGSp1")]
        smSGLp = [sem("smSGLp0"), sem("smSGLp1")]
        smHSqp = [sem("smHSqp0"), sem("smHSqp1")]
        smXLp = [sem("smXLp0"), sem("smXLp1")]
        smOUTp = [sem("smOUTp0"), sem("smOUTp1")]
        smVF = sem("smVF"); smAF = sem("smAF")
        _vfc = [0]; _afc = [0]

        def vsync(v, ins):
            ins.then_inc(smVF, 1)
            _vfc[0] += 1
            v.wait_ge(smVF, _vfc[0])

        def asyncf(s, ins):
            ins.then_inc(smAF, 1)
            _afc[0] += 1
            s.wait_ge(smAF, _afc[0])

        with nc.Block() as blk:
            @blk.sync
            def _(sy):
                pairs = [(ident, ident_d), (ident16, ident16_d), (ones_row, ones_d),
                         (vpc, vec_pc_d), (alpha_bc, alpha_bc_d), (bv_bc, bv_bc_d),
                         (clip2, clip2_d)]
                if "SETUP" in skip:
                    pairs = pairs[:2]

                pairs.append((sel2, sel2_d))
                for dst, srcp in pairs:
                    sy.dma_start(out=dst[:], in_=srcp).then_inc(smSET, 16)
                sy.wait_ge(smSET, 16 * len(pairs))

            @blk.vector
            def _(v):
                v.memset(gacc[:], 0.0)
                v.memset(hacc[:], 0.0)
                v.memset(cc8[:], 0.0)

            @blk.gpsimd
            def _(g):
                if "CAST" in skip:
                    return
                g.dma_start(out=wq16s, in_=Wq).then_inc(smWC, 16)
                g.dma_start(out=wk16s, in_=Wk).then_inc(smWC, 16)
                g.dma_start(out=wv16s, in_=Wv).then_inc(smWC, 16)
                g.dma_start(out=wo16s, in_=Wo).then_inc(smWC, 16)

        esXQ = contextlib.ExitStack()
        es.enter_context(esXQ)
        XQF = [esXQ.enter_context(nc.sbuf_tensor(f"XQF{i}", [128, NQ], BF16)) for i in range(DC)]
        with contextlib.ExitStack() as esAB:
            sbAB = lambda name, shape, dt=F32: esAB.enter_context(nc.sbuf_tensor(name, shape, dt))
            XN = [sbAB(f"XN{i}", [128, NKV], BF16) for i in range(DC)]

            with contextlib.ExitStack() as esA:
                sbA = lambda name, shape, dt=F32: esA.enter_context(nc.sbuf_tensor(name, shape, dt))
                xnq_tok = sbA("xnq_tok", [128, NT_Q * D])
                xa = [sbA(f"xa{i}", [128, D]) for i in range(2)]
                xnf0 = sbA("xnf0", [128, D])
                xnf = [xnf0, xnf0]
                sqscr = sbA("sqscr", [128, D])
                x16 = [sbA(f"x16_{i}", [128, D], BF16) for i in range(2)]
                stt = [sbA(f"stt{i}", [128, 8]) for i in range(2)]
                ptr = [esA.enter_context(nc.psum_tensor(f"ptrA{i}", [128, 128], BF16)) for i in range(4)]
                ptrf = esA.enter_context(nc.psum_tensor("ptrfA", [128, 128], F32))
                pbc = esA.enter_context(nc.psum_tensor("pbcA", [128, 4], F32))

                NTTe = (4 if "TINY" in skip else NT_KV) if "QTILES" in skip else NTT
                with nc.Block() as blk:
                    @blk.sync
                    def _(sy):
                        sy.wait_ge(smSET, 16 * (3 if "SETUP" in skip else 8))
                        for t in range(NTTe):
                            if t >= 2:
                                sy.wait_ge(smXR, t - 1)
                                sy.wait_ge(smSQ, t - 1)
                            src = xkv_t[t] if t < NT_KV else xq_t[t - NT_KV]
                            sy.dma_start(out=xa[t % 2][:], in_=src).then_inc(smXAp[t % 2], 16)
                        if "DUMP" not in skip:
                            sy.wait_ge(smEV, 8 * NTTe)
                            for c in range(DC):
                                sy.dma_start(out=xnqscr[c * 128:(c + 1) * 128, :],
                                             in_=XQF[c][:]).then_inc(smNQ, 16)

                    @blk.scalar
                    def _(s):
                        for t in range(NTTe):
                            s.wait_ge(smXAp[t % 2], 16 * (t // 2 + 1))
                            if t >= 1:
                                s.wait_ge(smVS, t)
                            s.activation(sqscr[:], xa[t % 2][:], AF.Square).then_inc(smSQ, 1)
                            s.wait_ge(smVS, t + 1)
                            asyncf(s, s.activation(stt[t % 2][:, 4:5], stt[t % 2][:, 2:3], AF.Ln))
                            s.activation(stt[t % 2][:, 4:5], stt[t % 2][:, 4:5], AF.Exp,
                                         scale=-0.5).then_inc(smRS, 1)

                    @blk.vector
                    def _(v):
                        for t in range(NTTe):
                            v.wait_ge(smXAp[t % 2], 16 * (t // 2 + 1))
                            st = stt[t % 2]
                            vsync(v, v.tensor_reduce(st[:, 0:1], xa[t % 2][:], axis=AX.X, op=ALU.add))
                            v.tensor_scalar(st[:, 0:1], st[:, 0:1], 1.0 / D, None, ALU.mult)
                            v.wait_ge(smSQ, t + 1)
                            vsync(v, v.tensor_reduce(st[:, 1:2], sqscr[:], axis=AX.X, op=ALU.add))
                            v.tensor_scalar(st[:, 2:3], st[:, 1:2], 1.0 / D, None, ALU.mult)
                            vsync(v, v.tensor_tensor(st[:, 3:4], st[:, 0:1], st[:, 0:1], ALU.mult))
                            vsync(v, v.tensor_tensor(st[:, 2:3], st[:, 2:3], st[:, 3:4], ALU.subtract))
                            v.tensor_scalar(st[:, 2:3], st[:, 2:3], float(EPS), None,
                                            ALU.add).then_inc(smVS, 1)
                            v.tensor_scalar(xnf[t % 2][:], xa[t % 2][:], st[:, 0:1], None,
                                            ALU.subtract).then_inc(smXR, 1)
                            v.wait_ge(smRS, t + 1)
                            v.tensor_scalar(xnf[t % 2][:], xnf[t % 2][:], st[:, 4:5], None, ALU.mult)
                            if t < NT_KV:
                                vsync(v, v.reduce_max(st[:, 5:6], xnf[t % 2][:], axis=AX.X,
                                                      apply_absolute_value=True))
                                v.tensor_max(gacc[:], gacc[:], st[:, 5:6])
                            else:
                                v.tensor_copy(xnq_tok[:, (t - NT_KV) * D:(t - NT_KV + 1) * D], xnf[t % 2][:])
                            if t >= 2:
                                v.wait_ge(smTR, 8 * (t - 1))
                            v.tensor_copy(x16[t % 2][:], xnf[t % 2][:]).then_inc(smCA, 1)
                            if t >= 1:
                                tp = t - 1
                                for c in range(DC):
                                    k = 8 * tp + c
                                    v.wait_ge(smTR, k + 1)
                                    dst = (XN[c][:, tp * 128:(tp + 1) * 128] if tp < NT_KV else
                                           XQF[c][:, (tp - NT_KV) * 128:(tp - NT_KV + 1) * 128])
                                    v.tensor_copy(dst, ptr[k % 4][:]).then_inc(smEV, 1)
                        tp = NTTe - 1
                        for c in range(DC):
                            k = 8 * tp + c
                            v.wait_ge(smTR, k + 1)
                            dstf = (XN[c][:, tp * 128:(tp + 1) * 128] if tp < NT_KV else
                                    XQF[c][:, (tp - NT_KV) * 128:(tp - NT_KV + 1) * 128])
                            v.tensor_copy(dstf, ptr[k % 4][:]).then_inc(smEV, 1)
                        if "GX" in skip:
                            v.memset(gmax_g[:], 1.0)
                            v.tensor_copy(cc8[0:1, 0:1], gmax_g[0:1, 0:1]).then_inc(smGM, 1)
                        else:
                            v.wait_ge(smTR, 8 * NTTe + 1)
                            vsync(v, v.reduce_max(gmax_g[0:1, 0:1], ptrf[0:1, 0:128], axis=AX.X))
                            v.tensor_copy(cc8[0:1, 0:1], gmax_g[0:1, 0:1]).then_inc(smGM, 1)

                    @blk.tensor
                    def _(t_):
                        for t in range(NTTe):
                            t_.wait_ge(smCA, t + 1)
                            for c in range(DC):
                                k = 8 * t + c
                                if k >= 4:
                                    t_.wait_ge(smEV, k - 3)
                                t_.transpose(ptr[k % 4][:], x16[t % 2][:, c * 128:(c + 1) * 128],
                                             ident16[:]).then_inc(smTR, 1)
                        if "GX" not in skip:
                            t_.wait_ge(smEV, 8 * NTTe)
                            t_.transpose(ptrf[0:1, 0:128], gacc[:], ident[:]).then_inc(smTR, 1)

                    @blk.gpsimd
                    def _(g):
                        if "GP" in skip:
                            return
                        g.wait_ge(smGM, 1)
                        g.dma_start(out=ccin1[None, :], in_=cc8[0:1, 0:8]).then_inc(smCC, 16)
                        g.wait_ge(smCC, 16)
                        if "CC" in skip:
                            g.dma_start(out=ccout1, in_=ccin1).then_inc(smCC, 16)
                        else:
                            g.collective_compute("AllReduce", ALU.max, replica_groups=[list(range(8))],
                                                 ins=[ccin1], outs=[ccout1]).then_inc(smCC, 1)
                        g.wait_ge(smCC, 17)
                        g.dma_start(out=gmax_g[0:1, 0:8], in_=ccout1[None, 0:8]).then_inc(smCC, 16)
                        g.wait_ge(smCC, 33)

                qv0 = sbA("qv0", [128, D])
                qv = [qv0, qv0]
                q16 = [sbA(f"q16_{i}", [128, D], BF16) for i in range(2)]

                if "A2B" not in skip:
                    with nc.Block() as blk:
                        @blk.scalar
                        def _(s):
                            if "A2" in skip:
                                return
                            s.activation(sig2[0:1, 0:2], clip2[0:1, 0:2], AF.Sigmoid).then_inc(smGM, 1)

                        @blk.vector
                        def _(v):
                            if "A2" in skip:
                                return
                            v.wait_ge(smGM, 2)
                            vsync(v, v.tensor_tensor(sc4[0:1, 1:2], sig2[0:1, 0:1], gmax_g[0:1, 0:1], ALU.mult))
                            vsync(v, v.tensor_scalar(sc4[0:1, 1:2], sc4[0:1, 1:2], 1.0 / 127.0, None, ALU.mult))
                            v.reciprocal(sc4[0:1, 0:1], sc4[0:1, 1:2]).then_inc(smGM, 1)
                            v.wait_ge(smBC, 1)
                            vsync(v, v.tensor_copy(isx_pc[:], pbc[:, 0:1]))
                            v.tensor_copy(sx_pc[:], pbc[:, 1:2]).then_inc(smGM, 1)
                            for t in range(NT_Q):
                                v.tensor_scalar(qv[t % 2][:], xnq_tok[:, t * D:(t + 1) * D], isx_pc[:, 0:1],
                                                None, ALU.mult)
                                v.tensor_scalar(qv[t % 2][:], qv[t % 2][:], MAGIC, None, ALU.add)
                                v.tensor_scalar(qv[t % 2][:], qv[t % 2][:], MAGIC, None, ALU.subtract)
                                if t >= 2:
                                    v.wait_ge(smT2, 8 * (t - 1))
                                v.tensor_scalar(q16[t % 2][:], qv[t % 2][:], 127.0, -128.0, ALU.min,
                                                ALU.max).then_inc(smQV, 1)
                                if t >= 1:
                                    tp = t - 1
                                    for c in range(DC):
                                        k = 8 * tp + c
                                        v.wait_ge(smT2, k + 1)
                                        v.tensor_copy(XQF[c][:, tp * 128:(tp + 1) * 128],
                                                      ptr[k % 4][:]).then_inc(smE2, 1)
                            tp = NT_Q - 1
                            for c in range(DC):
                                k = 8 * tp + c
                                v.wait_ge(smT2, k + 1)
                                v.tensor_copy(XQF[c][:, tp * 128:(tp + 1) * 128], ptr[k % 4][:]).then_inc(smE2, 1)

                        @blk.tensor
                        def _(t_):
                            if "A2" in skip:
                                return
                            t_.wait_ge(smGM, 3)
                            t_.matmul(pbc[:, 0:2], ones_row[0:1, :], sc4[0:1, 0:2], start=True,
                                      stop=True).then_inc(smBC, 1)
                            t_.wait_ge(smNQ, 16 * DC)
                            for t in range(NT_Q):
                                t_.wait_ge(smQV, t + 1)
                                for c in range(DC):
                                    k = 8 * t + c
                                    if k >= 4:
                                        t_.wait_ge(smE2, k - 3)
                                    t_.transpose(ptr[k % 4][:], q16[t % 2][:, c * 128:(c + 1) * 128],
                                                 ident16[:]).then_inc(smT2, 1)


                if "xn_fm" in dbg or "xq_fm" in dbg or "gscale" in dbg:
                    with nc.Block() as blk:
                        @blk.sync
                        def _(sy):
                            n = 0
                            if "xn_fm" in dbg:
                                dt_ = dbg["xn_fm"].rearrange("(c p) n -> c p n", p=128)
                                for c in range(DC):
                                    sy.dma_start(out=dt_[c], in_=XN[c][:]).then_inc(smDBG, 16)
                                    n += 16
                            if "xq_fm" in dbg:
                                dt_ = dbg["xq_fm"].rearrange("(c p) n -> c p n", p=128)
                                for c in range(DC):
                                    sy.dma_start(out=dt_[c], in_=XQF[c][:]).then_inc(smDBG, 16)
                                    n += 16
                            if "gscale" in dbg:
                                sy.dma_start(out=dbg["gscale"], in_=sc4[0:1, 0:4]).then_inc(smDBG, 16)
                                n += 16
                            sy.wait_ge(smDBG, n)

            if phases >= 2:
                with contextlib.ExitStack() as esB:
                    sbB = lambda name, shape, dt=F32: esB.enter_context(nc.sbuf_tensor(name, shape, dt))
                    WT = [sbB(f"WT{i}", [128, D], BF16) for i in range(DC)]

                    with contextlib.ExitStack() as esB1:
                        sbB1 = lambda name, shape, dt=F32: esB1.enter_context(nc.sbuf_tensor(name, shape, dt))
                        xnqr = [sbB1(f"xnqr{i}", [128, NQ], BF16) for i in range(DC)]
                        Qstg = [sbB1(f"Qstg{i}", [128, NQ], BF16) for i in range(2)]
                        pQ = [esB1.enter_context(nc.psum_tensor(f"pQ{i}", [128, 512], F32)) for i in range(2)]

                        with nc.Block() as blk:
                            @blk.sync
                            def _(sy):
                                sy.wait_ge(smWC, 64)
                                for i in range(DC):
                                    sy.dma_start(out=WT[i][:], in_=wq16s[0:D, i * 128:(i + 1) * 128],
                                                 transpose=True).then_inc(smWT, 16)
                                sy.wait_ge(smNQ, 16 * DC)
                                for i in range(DC):
                                    sy.dma_start(out=xnqr[i][:],
                                                 in_=xnqscr[i * 128:(i + 1) * 128, :]).then_inc(smXN2, 16)
                                if "B1X" in skip or "B1Y" in skip:
                                    return
                                for dc in range(DC):
                                    sy.wait_ge(smEB, 4 * (dc + 1))
                                    r0 = dc * 128
                                    sy.dma_start(out=qscr[r0:r0 + 64, :],
                                                 in_=Qstg[dc % 2][0:64, :]).then_inc(smSBp[dc % 2], 16)
                                    sy.dma_start(out=qscr[r0 + 64:r0 + 128, :],
                                                 in_=Qstg[dc % 2][64:128, :]).then_inc(smSBp[dc % 2], 16)

                            @blk.tensor
                            def _(t_):
                                if "B1X" in skip:
                                    return
                                t_.wait_ge(smWT, 16 * DC)
                                t_.wait_ge(smXN2, 16 * DC)
                                for dc in range(DC):
                                    for nch in range(4):
                                        g = dc * 4 + nch
                                        if g >= 2 and "B1Y" not in skip:
                                            t_.wait_ge(smEB, g - 1)
                                        for i in range(DC):
                                            mm = t_.matmul(pQ[g % 2][:, :], WT[i][:, dc * 128:(dc + 1) * 128],
                                                           xnqr[i][:, nch * 512:(nch + 1) * 512],
                                                           start=(i == 0), stop=(i == DC - 1))
                                        mm.then_inc(smPM, 1)

                            @blk.vector
                            def _(v):
                                if "B1X" in skip or "B1Y" in skip:
                                    return
                                for dc in range(DC):
                                    if dc >= 2:
                                        v.wait_ge(smSBp[dc % 2], 32 * (dc // 2))
                                    for nch in range(4):
                                        g = dc * 4 + nch
                                        v.wait_ge(smPM, g + 1)
                                        v.tensor_scalar(Qstg[dc % 2][:, nch * 512:(nch + 1) * 512],
                                                        pQ[g % 2][:, :], vpc[:, dc:dc + 1], None,
                                                        ALU.add).then_inc(smEB, 1)

                    if "B2" not in skip:
                      with contextlib.ExitStack() as esB2:
                        sbB2 = lambda name, shape, dt=F32: esB2.enter_context(nc.sbuf_tensor(name, shape, dt))
                        vstg = [sbB2(f"vstg{i}", [128, H * 65], BF16) for i in range(2)]
                        pV = [esB2.enter_context(nc.psum_tensor(f"pV{i}", [128, 512], F32)) for i in range(2)]

                        with nc.Block() as blk:
                            @blk.sync
                            def _(sy):
                                sy.wait_ge(smWC, 48)
                                for i in range(DC):
                                    sy.dma_start(out=WT[i][:], in_=wv16s[0:D, i * 128:(i + 1) * 128],
                                                 transpose=True).then_inc(smWT, 16)
                                for mt in range(NT_KV):
                                    sy.wait_ge(smEB, 32 + 3 * (mt + 1))
                                    sy.dma_start(out=vscr[mt * 128:(mt + 1) * 128, :],
                                                 in_=vstg[mt % 2][:]).then_inc(smSBvp[mt % 2], 16)

                            @blk.tensor
                            def _(t_):
                                t_.wait_ge(smWT, 16 * DC * 2)
                                for mt in range(NT_KV):
                                    for nch in range(2):
                                        if mt >= 1:
                                            t_.wait_ge(smEB, 32 + 3 * mt)
                                        for i in range(DC):
                                            mm = t_.matmul(pV[nch][:, :], XN[i][:, mt * 128:(mt + 1) * 128],
                                                           WT[i][:, nch * 512:(nch + 1) * 512],
                                                           start=(i == 0), stop=(i == DC - 1))
                                        mm.then_inc(smPM, 1)

                            @blk.vector
                            def _(v):
                                for mt in range(NT_KV):
                                    if mt >= 2:
                                        v.wait_ge(smSBvp[mt % 2], 16 * (mt // 2))
                                    vt = vstg[mt % 2][:].rearrange("p (h c) -> p h c", c=65)
                                    v.memset(vt[:, :, 64:65], 1.0).then_inc(smEB, 1)
                                    for nch in range(2):
                                        v.wait_ge(smPM, 32 + mt * 2 + nch + 1)
                                        src = pV[nch][:, :].rearrange("p (h c) -> p h c", c=64)
                                        bvs = bv_bc[:, nch * 512:(nch + 1) * 512].rearrange(
                                            "p (h c) -> p h c", c=64)
                                        v.tensor_tensor(vt[:, nch * 8:(nch + 1) * 8, 0:64], src, bvs,
                                                        ALU.add).then_inc(smEB, 1)

                    if "B3" not in skip:
                      with contextlib.ExitStack() as esB3:
                        sbB3 = lambda name, shape, dt=F32: esB3.enter_context(nc.sbuf_tensor(name, shape, dt))
                        Kstg = [sbB3(f"Kstg{i}", [128, NKV], BF16) for i in range(2)]
                        kmx = sbB3("kmx", [128, 1])
                        pK = [esB3.enter_context(nc.psum_tensor(f"pK{i}", [128, 512], F32)) for i in range(2)]

                        with nc.Block() as blk:
                            @blk.sync
                            def _(sy):
                                sy.wait_ge(smWC, 32)
                                for i in range(DC):
                                    sy.dma_start(out=WT[i][:], in_=wk16s[0:D, i * 128:(i + 1) * 128],
                                                 transpose=True).then_inc(smWT, 16)
                                for dc in range(DC):
                                    sy.wait_ge(smEB, 128 + 8 * (dc + 1))
                                    sy.dma_start(out=kscr[dc * 128:(dc + 1) * 128, :],
                                                 in_=Kstg[dc % 2][:]).then_inc(smSBp[dc % 2], 16)
                                    sy.wait_ge(smKXk, dc + 1)
                                    sy.dma_start(out=kmaxscr[dc:dc + 1, :], in_=kmx[:, 0:1]).then_inc(smKXs, 16)

                            @blk.tensor
                            def _(t_):
                                t_.wait_ge(smWT, 16 * DC * 3)
                                for dc in range(DC):
                                    for mch in range(8):
                                        g = dc * 8 + mch
                                        if g >= 2:
                                            t_.wait_ge(smEB, 128 + g - 1)
                                        for i in range(DC):
                                            mm = t_.matmul(pK[g % 2][:, :], WT[i][:, dc * 128:(dc + 1) * 128],
                                                           XN[i][:, mch * 512:(mch + 1) * 512],
                                                           start=(i == 0), stop=(i == DC - 1))
                                        mm.then_inc(smPM, 1)

                            @blk.vector
                            def _(v):
                                for dc in range(DC):
                                    if dc >= 2:
                                        v.wait_ge(smSBp[dc % 2], 128 + 16 * (dc // 2))
                                    for mch in range(8):
                                        g = dc * 8 + mch
                                        v.wait_ge(smPM, 96 + g + 1)
                                        v.tensor_scalar(Kstg[dc % 2][:, mch * 512:(mch + 1) * 512],
                                                        pK[g % 2][:, :], vpc[:, 8 + dc:9 + dc], None,
                                                        ALU.add).then_inc(smEB, 1)
                                    if dc >= 1:
                                        v.wait_ge(smKXs, 16 * dc)
                                    v.reduce_max(kmx[:], Kstg[dc % 2][:], axis=AX.X,
                                                 apply_absolute_value=True).then_inc(smKXk, 1)

                    if "B4" not in skip:
                      with contextlib.ExitStack() as esB4:
                        sbB4 = lambda name, shape, dt=F32: esB4.enter_context(nc.sbuf_tensor(name, shape, dt))
                        qb = [sbB4(f"qb{i}", [64, NQ], BF16) for i in range(2)]
                        absq = [sbB4(f"absq{i}", [64, NQ], BF16) for i in range(2)]
                        kmlf = [sbB4(f"kmlf{i}", [64, 1]) for i in range(2)]
                        kml16 = [sbB4(f"kml16{i}", [64, 1], BF16) for i in range(2)]
                        bmax = [sbB4(f"bmax{i}", [128, 1]) for i in range(2)]
                        absal = sbB4("absal", [128, 16])
                        pB = esB4.enter_context(nc.psum_tensor("pB", [128, 16], F32))
                        pT1 = esB4.enter_context(nc.psum_tensor("pT1", [128, 128], F32))
                        pC16 = esB4.enter_context(nc.psum_tensor("pC16", [128, 16], F32))

                        with nc.Block() as blk:
                            @blk.sync
                            def _(sy):
                                sy.wait_ge(smSBp[0], 32 * (DC // 2))
                                sy.wait_ge(smSBp[1], 32 * (DC // 2))
                                sy.wait_ge(smKXs, 16 * DC)
                                for h in range(H):
                                    if h >= 2:
                                        sy.wait_ge(smAB, h - 1)
                                        sy.wait_ge(smKM, h - 1)
                                    sy.dma_start(out=qb[h % 2][:],
                                                 in_=qscr[h * 64:(h + 1) * 64, :]).then_inc(smQLp[h % 2], 16)
                                    sy.dma_start(out=kmlf[h % 2][:],
                                                 in_=kmaxscr[h // 2:h // 2 + 1,
                                                             (h % 2) * 64:(h % 2) * 64 + 64]).then_inc(smQLp[h % 2], 16)

                            @blk.scalar
                            def _(s):
                                for h in range(H):
                                    s.wait_ge(smQLp[h % 2], 32 * (h // 2 + 1))
                                    s.activation(absq[h % 2][:], qb[h % 2][:], AF.Abs).then_inc(smAB, 1)
                                s.activation(absal[:], alpha_bc[:], AF.Abs).then_inc(smAB, 1)

                            @blk.vector
                            def _(v):
                                for h in range(H):
                                    v.wait_ge(smQLp[h % 2], 32 * (h // 2 + 1))
                                    v.tensor_copy(kml16[h % 2][:], kmlf[h % 2][:]).then_inc(smKM, 1)
                                    v.wait_ge(smBM, h + 1)
                                    v.reduce_max(bmax[h % 2][:], pB[:, 0:16], axis=AX.X).then_inc(smBRa, 1)
                                    v.wait_ge(smTRb, h + 1)
                                    v.reduce_max(c_row[0:1, h:h + 1], pT1[0:1, 0:128],
                                                 axis=AX.X).then_inc(smBRb, 1)
                                v.wait_ge(smBC, 2)
                                v.wait_ge(smAB, H + 1)
                                vsync(v, v.tensor_tensor(bias_pc[:], pC16[:, 0:16], absal[:], ALU.mult))
                                v.tensor_scalar(bias_pc[:], bias_pc[:], -0.0625, None, ALU.mult)
                                v.tensor_scalar(alpha_s[:], alpha_bc[:], 0.125, None,
                                                ALU.mult).then_inc(smBRb, 1)

                            @blk.tensor
                            def _(t_):
                                for h in range(H):
                                    t_.wait_ge(smAB, h + 1)
                                    t_.wait_ge(smKM, h + 1)
                                    if h >= 1:
                                        t_.wait_ge(smBRa, h)
                                    for nt in range(16):
                                        mm = t_.matmul(pB[:, nt:nt + 1],
                                                       absq[h % 2][:, nt * 128:(nt + 1) * 128],
                                                       kml16[h % 2][:], start=True, stop=True)
                                    mm.then_inc(smBM, 1)
                                    t_.wait_ge(smBRa, h + 1)
                                    if h >= 1:
                                        t_.wait_ge(smBRb, h)
                                    t_.transpose(pT1[0:1, 0:128], bmax[h % 2][:], ident[:]).then_inc(smTRb, 1)
                                t_.wait_ge(smBRb, H)
                                t_.matmul(pC16[:, 0:16], ones_row[0:1, :], c_row[0:1, :], start=True,
                                          stop=True).then_inc(smBC, 1)

        qp_uid = [0]

        def quant_pass(wsrc, dst_scr, swt, nchunks, C, sems, lb, qb):
            smLp, smQ, smSp = sems
            qp_uid[0] += 1
            u = qp_uid[0]
            with contextlib.ExitStack() as esq:
                sbq = lambda name, shape, dt=F32: esq.enter_context(nc.sbuf_tensor(name, shape, dt))
                wch = [sbq(f"wch{i}_{u}", [128, C]) for i in range(2)]
                vqs = sbq(f"vqs{u}", [128, C])
                w16o = [sbq(f"w16o{i}_{u}", [128, C], BF16) for i in range(2)]
                qs = sbq(f"qs{u}", [128, 4])

                with nc.Block() as blk:
                    @blk.sync
                    def _(sy):
                        for j in range(nchunks + 2):
                            if 2 <= j:
                                sy.wait_ge(smQ, qb + j - 1)
                            if j < nchunks:
                                sy.dma_start(out=wch[j % 2][:],
                                             in_=wsrc[j * 128:(j + 1) * 128, :]).then_inc(smLp[j % 2], 16)
                            if j >= 2:
                                js = j - 2
                                sy.dma_start(out=dst_scr[js * 128:(js + 1) * 128, :],
                                             in_=w16o[js % 2][:]).then_inc(smSp[js % 2], 16)

                    @blk.vector
                    def _(v):
                        for j in range(nchunks):
                            v.wait_ge(smLp[j % 2], lb[j % 2] + 16 * (j // 2 + 1))
                            vsync(v, v.reduce_max(qs[:, 0:1], wch[j % 2][:], axis=AX.X,
                                                  apply_absolute_value=True))
                            vsync(v, v.reciprocal(qs[:, 1:2], qs[:, 0:1]))
                            vsync(v, v.tensor_scalar(qs[:, 1:2], qs[:, 1:2], 127.0, None, ALU.mult))
                            v.tensor_scalar(swt[:, j:j + 1], qs[:, 0:1], 1.0 / 127.0, None, ALU.mult)
                            v.tensor_scalar(vqs[:], wch[j % 2][:], qs[:, 1:2], None, ALU.mult)
                            v.tensor_scalar(vqs[:], vqs[:], MAGIC, None, ALU.add)
                            v.tensor_scalar(vqs[:], vqs[:], MAGIC, None, ALU.subtract)
                            if j >= 2:
                                v.wait_ge(smSp[j % 2], lb[j % 2] + 16 * (j // 2))
                            v.tensor_copy(w16o[j % 2][:], vqs[:]).then_inc(smQ, 1)

        fuse_gate = phases >= 4
        if fuse_gate:
            quant_pass(wg_d, wgq_scr, swg, JC, D, (smWLp, smWQ, smWSp), [0, 0], 0)
            quant_pass(wu_d, wuq_scr, swu, JC, D, (smWLp, smWQ, smWSp), [176, 176], JC)
        WT_C = 16 * DC * 3 + (16 * 8 * 2 * JC if fuse_gate else 0)

        if phases >= 3:
            esCE = es.enter_context(contextlib.ExitStack())
            ATT = [esCE.enter_context(nc.sbuf_tensor(f"ATT{i}", [128, NQ], BF16)) for i in range(DC)]

            with contextlib.ExitStack() as esC:
                sbC = lambda name, shape, dt=F32: esC.enter_context(nc.sbuf_tensor(name, shape, dt))
                VAUG = [sbC(f"VAUG{i}", [128, H * 65], BF16) for i in range(NT_KV)]
                Khd = [sbC(f"Khd{i}", [64, NKV], BF16) for i in range(2)]
                Qhd = [sbC(f"Qhd{i}", [64, NQ], BF16) for i in range(2)]
                ptile = [sbC(f"ptile{i}", [128, 1024], BF16) for i in range(2)]
                dnst = [sbC(f"dnst{i}", [1, 1024]) for i in range(2)]
                WSG = [sbC(f"WSG{i}", [128, 1024], BF16) for i in range(2)]
                sgstg = [sbC(f"sgstg{i}", [128, NQ]) for i in range(2)]
                Sp = [esC.enter_context(nc.psum_tensor(f"Sp{i}", [128, 1024], F32)) for i in range(2)]
                AVp = esC.enter_context(nc.psum_tensor("AVp", [65, 1024], F32))
                pG = [esC.enter_context(nc.psum_tensor(f"pGc{i}", [128, 512], F32)) for i in range(2)]
                smKLh = [smKL0, smKL1]
                gate_slots = {}
                if fuse_gate:
                    gq = 0
                    for gi in range(H * 2 * 32):
                        if gi % 5 == 2 and gq < 8 * JC:
                            gate_slots[gi] = gq
                            gq += 1
                    assert gq == 8 * JC, gq

                with nc.Block() as blk:
                    @blk.sync
                    def _(sy):
                        sy.wait_ge(smSBvp[0], 16 * (NT_KV // 2))
                        sy.wait_ge(smSBvp[1], 16 * (NT_KV // 2))
                        sy.wait_ge(smSBp[0], 192)
                        sy.wait_ge(smSBp[1], 192)
                        for h in range(2):
                            r0 = h * 64
                            sy.dma_start(out=Khd[h % 2][:], in_=kscr[r0:r0 + 64, :]).then_inc(smKLh[h % 2], 16)
                            sy.dma_start(out=Qhd[h % 2][:], in_=qscr[r0:r0 + 64, :]).then_inc(smKLh[h % 2], 16)
                        for mt in range(NT_KV):
                            sy.dma_start(out=VAUG[mt][:],
                                         in_=vscr[mt * 128:(mt + 1) * 128, :]).then_inc(smVL, 16)
                        if fuse_gate:
                            sy.wait_ge(smWSp[0], 352)
                            sy.wait_ge(smWSp[1], 352)
                            for j in range(2 * JC + 2):
                                if j < 2 * JC:
                                    if j >= 2:
                                        sy.wait_ge(smGM2, 4 * (j - 1))
                                    wsrc = wgq_scr if j < JC else wuq_scr
                                    jj = j % JC
                                    for i in range(DC):
                                        sy.dma_start(
                                            out=WSG[j % 2][:, i * 128:(i + 1) * 128],
                                            in_=wsrc[jj * 128:(jj + 1) * 128, i * 128:(i + 1) * 128],
                                            transpose=True).then_inc(smWT, 16)
                                if j >= 2:
                                    js = j - 2
                                    dsc = sgscr if js < JC else upscr
                                    jj = js % JC
                                    sy.wait_ge(smGE, 4 * (js + 1))
                                    sy.dma_start(out=dsc[jj * 128:(jj + 1) * 128, :],
                                                 in_=sgstg[js % 2][:]).then_inc(smSGSp[js % 2], 16)

                    @blk.tensor
                    def _(t_):
                        t_.wait_ge(smVL, 16 * NT_KV)

                        def gate_group(gq):
                            j, nch = gq // 4, gq % 4
                            if nch == 0:
                                t_.wait_ge(smWT, 16 * DC * 3 + 16 * 8 * (j + 1))
                            if gq >= 2:
                                t_.wait_ge(smGE, gq - 1)
                            for i in range(DC):
                                mm = t_.matmul(pG[gq % 2][:, :], WSG[j % 2][:, i * 128:(i + 1) * 128],
                                               XQF[i][:, nch * 512:(nch + 1) * 512],
                                               start=(i == 0), stop=(i == DC - 1))
                            mm.then_inc(smGM2, 1)

                        for h in range(H):
                            t_.wait_ge(smKLh[h % 2], 32 * (h // 2 + 1))
                            for nf in range(2):
                                it = h * 2 + nf

                                def scores(m, h=h, nf=nf, it=it):
                                    k = it * 32 + m
                                    if k >= 2:
                                        t_.wait_ge(smEX, k - 1)
                                    for half in range(2):
                                        mm = t_.matmul(Sp[k % 2][:, half * 512:(half + 1) * 512],
                                                       Khd[h % 2][:, m * 128:(m + 1) * 128],
                                                       Qhd[h % 2][:, nf * 1024 + half * 512:
                                                                   nf * 1024 + (half + 1) * 512],
                                                       start=True, stop=True)
                                    mm.then_inc(smSC, 1)

                                t_.wait_ge(smNM, it)
                                scores(0)
                                for m in range(32):
                                    if m < 31:
                                        scores(m + 1)
                                    k = it * 32 + m
                                    t_.wait_ge(smEX, k + 1)
                                    for half in range(2):
                                        mm = t_.matmul(AVp[0:65, half * 512:(half + 1) * 512],
                                                       VAUG[m][:, h * 65:(h + 1) * 65],
                                                       ptile[k % 2][:, half * 512:(half + 1) * 512],
                                                       start=(m == 0), stop=(m == 31))
                                    mm.then_inc(smAV, 1)
                                    if k in gate_slots:
                                        gate_group(gate_slots[k])
                            t_.nop().then_inc(smHD, 1)

                    @blk.scalar
                    def _(s):
                        for it in range(H * 2):
                            h = it // 2
                            for m in range(32):
                                k = it * 32 + m
                                s.wait_ge(smSC, k + 1)
                                if k >= 2:
                                    s.wait_ge(smAV, k - 1)
                                s.activation(ptile[k % 2][:], Sp[k % 2][:, :], AF.Exp,
                                             bias=bias_pc[:, h:h + 1],
                                             scale=alpha_s[:, h:h + 1]).then_inc(smEX, 1)

                    @blk.vector
                    def _(v):
                        for it in range(H * 2):
                            h, nf = it // 2, it % 2
                            for gi in range(it * 32, (it + 1) * 32):
                                if gi in gate_slots:
                                    gq = gate_slots[gi]
                                    j, nch = gq // 4, gq % 4
                                    v.wait_ge(smGM2, gq + 1)
                                    if nch == 0 and j >= 2:
                                        v.wait_ge(smSGSp[j % 2], 16 * (j // 2))
                                    v.tensor_copy(sgstg[j % 2][:, nch * 512:(nch + 1) * 512],
                                                  pG[gq % 2][:, :]).then_inc(smGE, 1)
                            v.wait_ge(smAV, 32 * (it + 1))
                            c = h // 2
                            rows = slice((h % 2) * 64, (h % 2) * 64 + 64)
                            v.tensor_copy(ATT[c][rows, nf * 1024:(nf + 1) * 1024], AVp[0:64, :])
                            if it >= 2:
                                v.wait_ge(smBB, 16 * (it - 1))
                            v.tensor_copy(dnst[it % 2][:], AVp[64:65, 0:1024]).then_inc(smNM, 1)

                    @blk.gpsimd
                    def _(g):
                        for h in range(H):
                            for nf in range(2):
                                it = h * 2 + nf
                                g.wait_ge(smNM, it + 1)
                                g.dma_start(out=dscr[h:h + 1, nf * 1024:(nf + 1) * 1024],
                                            in_=dnst[it % 2][:]).then_inc(smBB, 16)
                            if h + 2 < H:
                                g.wait_ge(smHD, h + 1)
                                r0 = (h + 2) * 64
                                g.dma_start(out=Khd[h % 2][:],
                                            in_=kscr[r0:r0 + 64, :]).then_inc(smKLh[h % 2], 16)
                                g.dma_start(out=Qhd[h % 2][:],
                                            in_=qscr[r0:r0 + 64, :]).then_inc(smKLh[h % 2], 16)

        dstop = 99
        for s_ in skip:
            if s_.startswith("DSTOP"):
                dstop = int(s_[5:])
        if phases >= 4:
            esDE = es.enter_context(contextlib.ExitStack())
            FFN = [esDE.enter_context(nc.sbuf_tensor(f"FFN{i}", [128, NQ], BF16)) for i in range(DC)]

            with contextlib.ExitStack() as esD0:



                smSGSp2 = [smUM, smUE]
                if dstop >= 4:
                  with contextlib.ExitStack() as esD2:
                      sbD2 = lambda name, shape, dt=F32: esD2.enter_context(nc.sbuf_tensor(name, shape, dt))
                      upld = [sbD2(f"upld{i}", [128, NQ]) for i in range(2)]
                      upstg = [sbD2(f"upstg{i}", [128, NQ]) for i in range(2)]
                      sgld = [sbD2(f"sgld{i}", [128, NQ]) for i in range(2)]
                      h16s = [sbD2(f"h16s{i}", [128, NQ], BF16) for i in range(2)]
                      SCU = sbD2("SCU", [128, JC])
                      SCG2 = sbD2("SCG2", [128, JC])
                      sgsil = [sbD2(f"sgsil{i}", [128, NQ]) for i in range(2)]
                      hstat = sbD2("hstat", [128, 2])

                      with nc.Block() as blk:
                          @blk.sync
                          def _(sy):
                              for j in range(JC + 2):
                                  if 2 <= j:
                                      sy.wait_ge(smHP, j - 1)
                                  if j < JC:
                                      sy.wait_ge(smSGSp[j % 2], 16 * (j // 2 + 1))
                                      sy.dma_start(out=sgld[j % 2][:],
                                                   in_=sgscr[j * 128:(j + 1) * 128, :]).then_inc(smSGLp[j % 2], 16)
                                      sy.wait_ge(smSGSp[j % 2], 176 + 16 * (j // 2 + 1))
                                      sy.dma_start(out=upld[j % 2][:],
                                                   in_=upscr[j * 128:(j + 1) * 128, :]).then_inc(smSGLp[j % 2], 16)
                                  if j >= 2:
                                      js = j - 2
                                      sy.dma_start(out=hscr[js * 128:(js + 1) * 128, :],
                                                   in_=h16s[js % 2][:]).then_inc(smSGSp2[js % 2], 16)

                          @blk.scalar
                          def _(s):
                              s.wait_ge(smSCG, 1)
                              for j in range(JC):
                                  s.wait_ge(smSGLp[j % 2], 32 * (j // 2 + 1))
                                  if j >= 2:
                                      s.wait_ge(smHP, j - 1)
                                  s.activation(sgsil[j % 2][:], sgld[j % 2][:], AF.Silu,
                                               scale=SCG2[:, j:j + 1]).then_inc(smAB, 1)

                          @blk.vector
                          def _(v):
                              vsync(v, v.tensor_scalar(SCU[:], swu[:], sx_pc[:, 0:1], None, ALU.mult))
                              v.tensor_scalar(SCG2[:], swg[:], sx_pc[:, 0:1], None, ALU.mult).then_inc(smSCG, 1)
                              for j in range(JC):
                                  v.wait_ge(smAB, H + 1 + j + 1)
                                  v.tensor_scalar(upstg[j % 2][:], upld[j % 2][:], SCU[:, j:j + 1],
                                                  None, ALU.mult)
                                  if j >= 2:
                                      v.wait_ge(smSGSp2[j % 2], 16 * (j // 2))
                                  v.tensor_tensor(h16s[j % 2][:], sgsil[j % 2][:], upstg[j % 2][:], ALU.mult)
                                  vsync(v, v.reduce_max(hstat[:, 0:1], h16s[j % 2][:], axis=AX.X,
                                                        apply_absolute_value=True))
                                  v.tensor_max(hacc[:], hacc[:], hstat[:, 0:1]).then_inc(smHP, 1)

                if dstop >= 6:
                    quant_pass(wd_d, wdq_scr, swd, DC, II, (smWLp, smWQ, smWSp), [352, 352], 2 * JC)

                if dstop >= 5:
                  with contextlib.ExitStack() as esD3:
                      ptrD = esD3.enter_context(nc.psum_tensor("ptrD", [128, 128], F32))
                      pbcD = esD3.enter_context(nc.psum_tensor("pbcD", [128, 4], F32))

                      with nc.Block() as blk:
                          @blk.tensor
                          def _(t_):
                              t_.transpose(ptrD[0:1, 0:128], hacc[:], ident[:]).then_inc(smGM, 1)
                              t_.wait_ge(smGM, 7)
                              t_.matmul(pbcD[:, 0:2], ones_row[0:1, :], sc4[0:1, 2:4], start=True,
                                        stop=True).then_inc(smBC, 1)

                          @blk.vector
                          def _(v):
                              v.wait_ge(smGM, 5)
                              vsync(v, v.reduce_max(hmax_g[0:1, 0:1], ptrD[0:1, 0:128], axis=AX.X))
                              v.tensor_copy(cc8[0:1, 0:1], hmax_g[0:1, 0:1]).then_inc(smGM, 1)
                              v.wait_ge(smCC, 66)
                              vsync(v, v.tensor_tensor(sc4[0:1, 3:4], sig2[0:1, 1:2], hmax_g[0:1, 0:1], ALU.mult))
                              vsync(v, v.tensor_scalar(sc4[0:1, 3:4], sc4[0:1, 3:4], 1.0 / 127.0, None, ALU.mult))
                              v.reciprocal(sc4[0:1, 2:3], sc4[0:1, 3:4]).then_inc(smGM, 1)
                              v.wait_ge(smBC, 3)
                              vsync(v, v.tensor_copy(is2_pc[:], pbcD[:, 0:1]))
                              v.tensor_copy(s2_pc[:], pbcD[:, 1:2]).then_inc(smGM, 1)

                          @blk.gpsimd
                          def _(g):
                              g.wait_ge(smGM, 6)
                              g.dma_start(out=ccin2[None, :], in_=cc8[0:1, 0:8]).then_inc(smCC, 16)
                              g.wait_ge(smCC, 49)
                              g.collective_compute("AllReduce", ALU.max, replica_groups=[list(range(8))],
                                                   ins=[ccin2], outs=[ccout2]).then_inc(smCC, 1)
                              g.wait_ge(smCC, 50)
                              g.dma_start(out=hmax_g[0:1, 0:8], in_=ccout2[None, 0:8]).then_inc(smCC, 16)
                              g.wait_ge(smCC, 66)

                if dstop >= 6:
                  with contextlib.ExitStack() as esDq:
                      sbDq = lambda name, shape, dt=F32: esDq.enter_context(nc.sbuf_tensor(name, shape, dt))
                      hld = [sbDq(f"hld{i}", [128, NQ], BF16) for i in range(2)]
                      hvq = sbDq("hvq", [128, NQ])
                      hqo = [sbDq(f"hqo{i}", [128, NQ], BF16) for i in range(2)]

                      with nc.Block() as blk:
                          @blk.sync
                          def _(sy):
                              sy.wait_ge(smSGSp2[0], 176)
                              sy.wait_ge(smSGSp2[1], 176)
                              for j in range(JC + 2):
                                  if 2 <= j:
                                      sy.wait_ge(smHQ, j - 1)
                                  if j < JC:
                                      sy.dma_start(out=hld[j % 2][:],
                                                   in_=hscr[j * 128:(j + 1) * 128, :]).then_inc(smSGLp[j % 2], 16)
                                  if j >= 2:
                                      js = j - 2
                                      sy.dma_start(out=hqscr[js * 128:(js + 1) * 128, :],
                                                   in_=hqo[js % 2][:]).then_inc(smHSqp[js % 2], 16)

                          @blk.vector
                          def _(v):
                              for j in range(JC):
                                  v.wait_ge(smSGLp[j % 2], 352 + 16 * (j // 2 + 1))
                                  v.tensor_scalar(hvq[:], hld[j % 2][:], is2_pc[:, 0:1], None, ALU.mult)
                                  v.tensor_scalar(hvq[:], hvq[:], MAGIC, None, ALU.add)
                                  v.tensor_scalar(hvq[:], hvq[:], MAGIC, None, ALU.subtract)
                                  if j >= 2:
                                      v.wait_ge(smHSqp[j % 2], 16 * (j // 2))
                                  v.tensor_scalar(hqo[j % 2][:], hvq[:], 127.0, -128.0, ALU.min,
                                                  ALU.max).then_inc(smHQ, 1)

                if dstop >= 7:
                  with contextlib.ExitStack() as esD4:
                      sbD4 = lambda name, shape, dt=F32: esD4.enter_context(nc.sbuf_tensor(name, shape, dt))
                      WDT = [sbD4(f"WDT{i}", [128, D], BF16) for i in range(JC)]
                      hq2 = [sbD4(f"hq2_{i}", [128, NQ], BF16) for i in range(2)]
                      SCD = sbD4("SCD", [128, DC])
                      pD2 = [esD4.enter_context(nc.psum_tensor(f"pD2_{i}", [128, 512], F32))
                             for i in range(8)]

                      with nc.Block() as blk:
                          @blk.sync
                          def _(sy):
                              sy.wait_ge(smWSp[0], 416)
                              sy.wait_ge(smWSp[1], 416)
                              if "D4Z" not in skip:
                                  for j in range(JC):
                                      sy.dma_start(out=WDT[j][:], in_=wdq_scr[0:D, j * 128:(j + 1) * 128],
                                                   transpose=True).then_inc(smWT, 16)
                              sy.wait_ge(smHSqp[0], 16 * 11)
                              sy.wait_ge(smHSqp[1], 16 * 11)
                              if "D4W" in skip:
                                  return
                              for hcg in range(4):
                                  if "D4X" in skip and hcg > 0:
                                      break
                                  for j in range(JC):
                                      kk = hcg * JC + j
                                      if kk >= 2 and "D4X" not in skip:
                                          sy.wait_ge(smDM, kk - 1)
                                      sy.dma_start(out=hq2[kk % 2][:],
                                                   in_=hqscr[j * 128:(j + 1) * 128, :]).then_inc(smSGLp[kk % 2], 16)

                          @blk.tensor
                          def _(t_):
                              t_.wait_ge(smWT, WT_C + 16 * JC)
                              for hcg in range(4):
                                  if hcg >= 1:
                                      t_.wait_ge(smDE, 8 * hcg)
                                  for j in range(JC):
                                      kk = hcg * JC + j
                                      t_.wait_ge(smSGLp[kk % 2], 528 + 16 * (kk // 2 + 1))
                                      for hc2 in range(2):
                                          hc = hcg * 2 + hc2
                                          for nch in range(4):
                                              mm = t_.matmul(pD2[hc2 * 4 + nch][:, :],
                                                             WDT[j][:, hc * 128:(hc + 1) * 128],
                                                             hq2[kk % 2][:, nch * 512:(nch + 1) * 512],
                                                             start=(j == 0), stop=(j == JC - 1))
                                      mm.then_inc(smDM, 1)

                          @blk.vector
                          def _(v):
                              if "D4X" in skip or "D4Y" in skip:
                                  return
                              vsync(v, v.tensor_scalar(SCD[:], swd[:], s2_pc[:, 0:1], None, ALU.mult))
                              for hcg in range(4):
                                  v.wait_ge(smDM, JC * (hcg + 1))
                                  for hc2 in range(2):
                                      hc = hcg * 2 + hc2
                                      for nch in range(4):
                                          v.tensor_scalar(FFN[hc][:, nch * 512:(nch + 1) * 512],
                                                          pD2[hc2 * 4 + nch][:, :], SCD[:, hc:hc + 1],
                                                          None, ALU.mult).then_inc(smDE, 1)

        if phases >= 5:
            with contextlib.ExitStack() as esE:
                sbE = lambda name, shape, dt=F32: esE.enter_context(nc.sbuf_tensor(name, shape, dt))
                WOT = [sbE(f"WOT{i}", [128, D], BF16) for i in range(DC)]
                gm_bc = sbE("gm_bcS", [128, D])
                bt_bc = sbE("bt_bcS", [128, D])
                XNQR = XQF
                yfm = [sbE(f"yfm{i}", [128, 512]) for i in range(DC)]
                ytok = [sbE(f"ytok{i}", [128, D]) for i in range(4)]
                sqE = sbE("sqE", [128, D])
                stE = sbE("stE", [128, 8])
                yln = sbE("yln", [128, D])
                attnb = sbE("attnb", [128, D])
                fftok = sbE("fftok", [128, D], BF16)
                xts = [sbE(f"xts{i}", [128, D]) for i in range(2)]
                outstg = [sbE(f"outstg{i}", [128, D]) for i in range(2)]
                drec = sbE("drec", [128, 256])
                drow2 = sbE("drow2", [2, NQ])
                recn = sbE("recn", [128, 512])
                pBC2 = esE.enter_context(nc.psum_tensor("pBC2", [128, 512], F32))
                pY = [esE.enter_context(nc.psum_tensor(f"pY{i}", [128, 512], F32)) for i in range(2)]
                ptE = [esE.enter_context(nc.psum_tensor(f"ptE{i}", [128, 128], F32)) for i in range(2)]
                ptF = [esE.enter_context(nc.psum_tensor(f"ptF{i}", [128, 128], BF16)) for i in range(2)]

                with nc.Block() as blk:
                    @blk.sync
                    def _(sy):
                        sy.wait_ge(smWC, 64)
                        sy.dma_start(out=gm_bc[:], in_=gm_bc_d).then_inc(smSET, 16)
                        sy.dma_start(out=bt_bc[:], in_=bt_bc_d).then_inc(smSET, 16)
                        if "ENON" not in skip:
                            sy.wait_ge(smBB, 16 * H * 2)
                            sy.dma_start(out=drec[:], in_=dscr.rearrange("h n -> (h n)").rearrange(
                                "(p f) -> p f", p=128)).then_inc(smRC, 16)
                            sy.wait_ge(smRC, 17)
                            sy.dma_start(out=dscr.rearrange("h n -> (h n)").rearrange(
                                "(p f) -> p f", p=128), in_=drec[:]).then_inc(smRC, 16)
                            for c in range(DC):
                                sy.wait_ge(smRC, 33)
                                if c >= 1:
                                    sy.wait_ge(smNM, H * 2 + 4 * c)
                                sy.dma_start(out=drow2[:], in_=dscr[2 * c:2 * c + 2, :]).then_inc(smBB, 16)
                        for i in range(DC):
                            sy.dma_start(out=WOT[i][:], in_=wo16s[0:D, i * 128:(i + 1) * 128],
                                         transpose=True).then_inc(smWT, 16)
                        for i in range(DC):
                            sy.dma_start(out=XNQR[i][:],
                                         in_=xnqscr[i * 128:(i + 1) * 128, :]).then_inc(smXQL, 16)
                        for nt in range(NT_Q + 2):
                            if 2 <= nt:
                                sy.wait_ge(smLN2, nt - 1)
                            if nt < NT_Q:
                                sy.dma_start(out=xts[nt % 2][:], in_=xq_t[nt]).then_inc(smXLp[nt % 2], 16)
                            if nt >= 2:
                                ns = nt - 2
                                sy.wait_ge(smOUTv, ns + 1)
                                sy.dma_start(out=out_t[ns], in_=outstg[ns % 2][:]).then_inc(smOUTp[ns % 2], 16)
                        sy.wait_ge(smOUTp[0], 16 * (NT_Q // 2))
                        sy.wait_ge(smOUTp[1], 16 * (NT_Q // 2))

                    @blk.tensor
                    def _(t_):
                        if "ENON" not in skip:
                            for c in range(DC):
                                for nch in range(4):
                                    kb = c * 4 + nch
                                    t_.wait_ge(smBB, 16 * H * 2 + 16 * (c + 1))
                                    if kb >= 1:
                                        t_.wait_ge(smNM, H * 2 + kb)
                                    mm = t_.matmul(pBC2[:, :], sel2[:, :],
                                                   drow2[0:2, nch * 512:(nch + 1) * 512],
                                                   start=True, stop=True)
                                    mm.then_inc(smRC, 1)
                            t_.wait_ge(smNM, H * 2 + 32)
                        t_.wait_ge(smWT, WT_C + 16 * JC + 16 * DC)
                        t_.wait_ge(smXQL, 16 * DC)
                        for nch in range(4):
                            for oc in range(DC):
                                g = nch * 8 + oc
                                if g >= 2:
                                    t_.wait_ge(smYE, g - 1)
                                for hd in range(DC):
                                    mm = t_.matmul(pY[g % 2][:, :], WOT[hd][:, oc * 128:(oc + 1) * 128],
                                                   ATT[hd][:, nch * 512:(nch + 1) * 512],
                                                   start=(hd == 0), stop=(hd == DC - 1))
                                mm.then_inc(smYM, 1)
                            for s4 in range(4):
                                nt = nch * 4 + s4
                                for oc in range(DC):
                                    kk = nt * 8 + oc
                                    t_.wait_ge(smYE, nch * 8 + oc + 1)
                                    if kk >= 2:
                                        t_.wait_ge(smYV, kk - 1)
                                    t_.transpose(ptE[kk % 2][:], yfm[oc][:, s4 * 128:(s4 + 1) * 128],
                                                 ident[:]).then_inc(smYT, 1)
                                for c2 in range(DC):
                                    kf = nt * 8 + c2
                                    if kf >= 2:
                                        t_.wait_ge(smFV, kf - 1)
                                    t_.transpose(ptF[kf % 2][:], FFN[c2][:, nt * 128:(nt + 1) * 128],
                                                 ident16[:]).then_inc(smFT, 1)

                    @blk.scalar
                    def _(s):
                        for nt in range(NT_Q):
                            s.wait_ge(smYV, 8 * (nt + 1))
                            s.activation(sqE[:], ytok[nt % 4][:], AF.Square).then_inc(smSQ, 1)
                            s.wait_ge(smVS, NTT + nt + 1)
                            asyncf(s, s.activation(stE[:, 4:5], stE[:, 2:3], AF.Ln))
                            s.activation(stE[:, 4:5], stE[:, 4:5], AF.Exp, scale=-0.5).then_inc(smRS, 1)

                    @blk.vector
                    def _(v):
                        if "ENON" not in skip:
                            v.wait_ge(smRC, 16)
                            v.reciprocal(drec[:], drec[:]).then_inc(smRC, 1)
                            for c in range(DC):
                                for nch in range(4):
                                    kb = c * 4 + nch
                                    v.wait_ge(smRC, 33 + kb + 1)
                                    v.tensor_copy(recn[:], pBC2[:, :])
                                    v.tensor_tensor(ATT[c][:, nch * 512:(nch + 1) * 512],
                                                    ATT[c][:, nch * 512:(nch + 1) * 512], recn[:],
                                                    ALU.mult).then_inc(smNM, 1)
                        for nch in range(4):
                            for oc in range(DC):
                                g = nch * 8 + oc
                                v.wait_ge(smYM, g + 1)
                                if nch >= 1:
                                    v.wait_ge(smYT, (4 * nch - 1) * 8 + oc + 1)
                                v.tensor_scalar(yfm[oc][:], pY[g % 2][:, :], vpc[:, 16 + oc:17 + oc],
                                                None, ALU.add)
                                v.tensor_tensor(yfm[oc][:], yfm[oc][:],
                                                XNQR[oc][:, nch * 512:(nch + 1) * 512],
                                                ALU.add).then_inc(smYE, 1)
                            for s4 in range(4):
                                nt = nch * 4 + s4
                                for oc in range(DC):
                                    kk = nt * 8 + oc
                                    v.wait_ge(smYT, kk + 1)
                                    v.tensor_copy(ytok[s4][:, oc * 128:(oc + 1) * 128],
                                                  ptE[kk % 2][:]).then_inc(smYV, 1)
                                v.wait_ge(smSQ, NTT + nt + 1)
                                vsync(v, v.tensor_reduce(stE[:, 0:1], ytok[s4][:], axis=AX.X, op=ALU.add))
                                v.tensor_scalar(stE[:, 0:1], stE[:, 0:1], 1.0 / D, None, ALU.mult)
                                vsync(v, v.tensor_reduce(stE[:, 1:2], sqE[:], axis=AX.X, op=ALU.add))
                                v.tensor_scalar(stE[:, 2:3], stE[:, 1:2], 1.0 / D, None, ALU.mult)
                                vsync(v, v.tensor_tensor(stE[:, 3:4], stE[:, 0:1], stE[:, 0:1], ALU.mult))
                                vsync(v, v.tensor_tensor(stE[:, 2:3], stE[:, 2:3], stE[:, 3:4], ALU.subtract))
                                v.tensor_scalar(stE[:, 2:3], stE[:, 2:3], float(EPS), None,
                                                ALU.add).then_inc(smVS, 1)
                                v.wait_ge(smRS, NTT + nt + 1)
                                v.wait_ge(smSET, 16 * 10)
                                v.tensor_scalar(yln[:], ytok[s4][:], stE[:, 0:1], None, ALU.subtract)
                                v.tensor_scalar(yln[:], yln[:], stE[:, 4:5], None, ALU.mult)
                                v.tensor_tensor(attnb[:], yln[:], gm_bc[:], ALU.mult)
                                v.tensor_tensor(attnb[:], attnb[:], bt_bc[:], ALU.add)
                                v.wait_ge(smXLp[nt % 2], 16 * (nt // 2 + 1))
                                v.tensor_tensor(attnb[:], attnb[:], xts[nt % 2][:],
                                                ALU.add).then_inc(smLN2, 1)
                                for c2 in range(DC):
                                    kf = nt * 8 + c2
                                    v.wait_ge(smFT, kf + 1)
                                    v.tensor_copy(fftok[:, c2 * 128:(c2 + 1) * 128],
                                                  ptF[kf % 2][:]).then_inc(smFV, 1)
                                if nt >= 2:
                                    v.wait_ge(smOUTp[nt % 2], 16 * (nt // 2))
                                v.tensor_tensor(outstg[nt % 2][:], attnb[:], fftok[:],
                                                ALU.add).then_inc(smOUTv, 1)

        if dbg:
            with nc.Block() as blk:
                @blk.gpsimd
                def _(g):
                    n = 0
                    for name, src in [("kscr", kscr), ("qscr", qscr), ("vscr", vscr),
                                      ("hscr", hscr), ("xnqscr", xnqscr)]:
                        if name in dbg:
                            g.dma_start(out=dbg[name], in_=src).then_inc(smDBG, 16)
                            n += 16
                    if "crow" in dbg:
                        g.dma_start(out=dbg["crow"], in_=c_row[0:1, :]).then_inc(smDBG, 16)
                        n += 16
                    if "biaspc" in dbg:
                        g.dma_start(out=dbg["biaspc"], in_=bias_pc[:]).then_inc(smDBG, 16)
                        n += 16
                    if "attn" in dbg and phases >= 3:
                        dt_ = dbg["attn"].rearrange("(c p) n -> c p n", p=128)
                        for c in range(DC):
                            g.dma_start(out=dt_[c], in_=ATT[c][:]).then_inc(smDBG, 16)
                            n += 16
                    if "ffn" in dbg and phases >= 4:
                        dt_ = dbg["ffn"].rearrange("(c p) n -> c p n", p=128)
                        for c in range(DC):
                            g.dma_start(out=dt_[c], in_=FFN[c][:]).then_inc(smDBG, 16)
                            n += 16
                    g.wait_ge(smDBG, n)

        if phases < 5 and "STUB" not in skip:
            with nc.sbuf_tensor("zstub", [128, D], F32) as zt:
                with nc.Block() as blk:
                    @blk.vector
                    def _(v):
                        v.memset(zt[:], 0.0).then_inc(smOUTv, 1)

                    @blk.sync
                    def _(sy):
                        sy.wait_ge(smOUTv, 1)
                        for t in range(NT_Q):
                            sy.dma_start(out=out_t[t], in_=zt[:]).then_inc(smOUT, 16)
                        sy.wait_ge(smOUT, 16 * NT_Q)

    return nc, dbg



def _host_inputs(inputs, core):
    b, qh = core // 2, core % 2
    x = np.asarray(inputs["L0_attn_x"])
    bq = np.asarray(inputs["L0_attn_bq"]); bk = np.asarray(inputs["L0_attn_bk"])
    bo = np.asarray(inputs["L0_attn_bo"])
    vec_pc = np.empty((128, 24), np.float32)
    for vi, arr in enumerate([bq, bk, bo]):
        vec_pc[:, vi * 8:(vi + 1) * 8] = arr.reshape(8, 128).T
    return dict(
        x_kv=np.ascontiguousarray(x[b]),
        x_q=np.ascontiguousarray(x[b, qh * NQ:(qh + 1) * NQ]),
        Wq=np.asarray(inputs["L0_attn_Wq"]), Wk=np.asarray(inputs["L0_attn_Wk"]),
        Wv=np.asarray(inputs["L0_attn_Wv"]), Wo=np.asarray(inputs["L0_attn_Wo"]),
        wg=np.asarray(inputs["L0_ffn_w_gate"]), wu=np.asarray(inputs["L0_ffn_w_up"]),
        wd=np.asarray(inputs["L0_ffn_w_down"]),
        vec_pc=vec_pc,
        alpha_bc=np.ascontiguousarray(np.broadcast_to(
            np.asarray(inputs["L0_attn_adaptive_scale"])[None, :], (128, 16))).astype(np.float32),
        bv_bc=np.ascontiguousarray(np.broadcast_to(
            np.asarray(inputs["L0_attn_bv"])[None, :], (128, D))).astype(np.float32),
        gm_bc=np.ascontiguousarray(np.broadcast_to(
            np.asarray(inputs["L0_attn_gamma"])[None, :], (128, D))).astype(np.float32),
        bt_bc=np.ascontiguousarray(np.broadcast_to(
            np.asarray(inputs["L0_attn_beta"])[None, :], (128, D))).astype(np.float32),
        clip2=np.array([[float(np.asarray(inputs["L0_ffn_clip_x"])[0]),
                         float(np.asarray(inputs["L0_ffn_clip_down"])[0])] + [0.0] * 6], np.float32),
        ident=np.eye(128, dtype=np.float32),
        sel2=np.concatenate([np.repeat([[1.0, 0.0]], 64, 0), np.repeat([[0.0, 1.0]], 64, 0)],
                            0).T.astype(np.float32).copy(),
        ident16=np.eye(128, dtype=ml_dtypes.bfloat16),
        ones_row=np.ones((1, 128), np.float32),
    )


def kernel(**inputs):
    nc, _ = build_nc(phases=5)
    in_maps = [_host_inputs(inputs, c) for c in range(8)]
    res = run_bass_kernel_spmd(nc, in_maps, core_ids=list(range(8)))
    out = np.empty((4, NKV, D), np.float32)
    for c in range(8):
        b, qh = c // 2, c % 2
        out[b, qh * NQ:(qh + 1) * NQ] = res.results[c]["out"]
    return out


# version 05
# speedup vs baseline: 1.0830x; 1.0830x over previous
import contextlib
import numpy as np
import ml_dtypes
import concourse.bass as bass
import concourse.mybir as mybir
from concourse.bass_utils import run_bass_kernel_spmd

F32 = mybir.dt.float32
BF16 = mybir.dt.bfloat16
AF = mybir.ActivationFunctionType
ALU = mybir.AluOpType
AX = mybir.AxisListType

MAGIC = 1.5 * 2 ** 23
EPS = 1e-5
D = 1024
NKV = 4096
NQ = 2048
H = 16
II = 2816
NT_KV = 32
NT_Q = 16
NTT = 48
DC = 8
JC = 22


def build_nc(phases=5, debug=(), skip=()):
    nc = bass.Bass("TRN2", target_bir_lowering=False)
    ap = lambda t: t.ap()

    x_kv = ap(nc.dram_tensor("x_kv", [NKV, D], F32, kind="ExternalInput"))
    x_q = ap(nc.dram_tensor("x_q", [NQ, D], F32, kind="ExternalInput"))
    Wq = ap(nc.dram_tensor("Wq", [D, D], F32, kind="ExternalInput"))
    Wk = ap(nc.dram_tensor("Wk", [D, D], F32, kind="ExternalInput"))
    Wv = ap(nc.dram_tensor("Wv", [D, D], F32, kind="ExternalInput"))
    Wo = ap(nc.dram_tensor("Wo", [D, D], F32, kind="ExternalInput"))
    wg_d = ap(nc.dram_tensor("wg", [II, D], F32, kind="ExternalInput"))
    wu_d = ap(nc.dram_tensor("wu", [II, D], F32, kind="ExternalInput"))
    wd_d = ap(nc.dram_tensor("wd", [D, II], F32, kind="ExternalInput"))
    vec_pc_d = ap(nc.dram_tensor("vec_pc", [128, 24], F32, kind="ExternalInput"))
    alpha_bc_d = ap(nc.dram_tensor("alpha_bc", [128, 16], F32, kind="ExternalInput"))
    bv_bc_d = ap(nc.dram_tensor("bv_bc", [128, D], F32, kind="ExternalInput"))
    gm_bc_d = ap(nc.dram_tensor("gm_bc", [128, D], F32, kind="ExternalInput"))
    bt_bc_d = ap(nc.dram_tensor("bt_bc", [128, D], F32, kind="ExternalInput"))
    clip2_d = ap(nc.dram_tensor("clip2", [1, 8], F32, kind="ExternalInput"))
    ident_d = ap(nc.dram_tensor("ident", [128, 128], F32, kind="ExternalInput"))
    ident16_d = ap(nc.dram_tensor("ident16", [128, 128], BF16, kind="ExternalInput"))
    ones_d = ap(nc.dram_tensor("ones_row", [1, 128], F32, kind="ExternalInput"))
    sel2_d = ap(nc.dram_tensor("sel2", [2, 128], F32, kind="ExternalInput"))
    out_d = ap(nc.dram_tensor("out", [NQ, D], F32, kind="ExternalOutput"))

    dbg = {}
    for name, shape, dt in [("xn_fm", [D, NKV], BF16), ("xq_fm", [D, NQ], BF16),
                            ("gscale", [1, 4], F32), ("kscr", [D, NKV], BF16),
                            ("qscr", [D, NQ], BF16), ("vscr", [NKV, H * 65], BF16),
                            ("crow", [1, 16], F32), ("biaspc", [128, 16], F32),
                            ("attn", [D, NQ], BF16), ("hscr", [II, NQ], BF16),
                            ("ffn", [D, NQ], BF16), ("xnqscr", [D, NQ], BF16)]:
        if name in debug:
            dbg[name] = ap(nc.dram_tensor("dbg_" + name, shape, dt, kind="ExternalOutput"))

    wq16s = ap(nc.dram_tensor("wq16s", [D, D], BF16))
    wk16s = ap(nc.dram_tensor("wk16s", [D, D], BF16))
    wv16s = ap(nc.dram_tensor("wv16s", [D, D], BF16))
    wo16s = ap(nc.dram_tensor("wo16s", [D, D], BF16))
    kscr = ap(nc.dram_tensor("kscr", [D, NKV], BF16))
    qscr = ap(nc.dram_tensor("qscr", [D, NQ], BF16))
    xnqscr = ap(nc.dram_tensor("xnqscr", [D, NQ], BF16))
    vscr = ap(nc.dram_tensor("vscr", [NKV, H * 65], BF16))
    kmaxscr = ap(nc.dram_tensor("kmaxscr", [DC, 128], F32))
    wgq_scr = ap(nc.dram_tensor("wgq_scr", [II, D], BF16))
    wuq_scr = ap(nc.dram_tensor("wuq_scr", [II, D], BF16))
    wdq_scr = ap(nc.dram_tensor("wdq_scr", [D, II], BF16))
    sgscr = ap(nc.dram_tensor("sgscr", [II, NQ], F32))
    hscr = ap(nc.dram_tensor("hscr", [II, NQ], BF16))
    hqscr = ap(nc.dram_tensor("hqscr", [II, NQ], BF16))
    dscr = ap(nc.dram_tensor("dscr", [H, NQ], F32))
    ccin1 = ap(nc.dram_tensor("ccin1", [8], F32))
    ccout1 = ap(nc.dram_tensor("ccout1", [8], F32))
    ccin2 = ap(nc.dram_tensor("ccin2", [8], F32))
    ccout2 = ap(nc.dram_tensor("ccout2", [8], F32))

    xkv_t = x_kv.rearrange("(t p) d -> t p d", p=128)
    xq_t = x_q.rearrange("(t p) d -> t p d", p=128)
    out_t = out_d.rearrange("(t p) d -> t p d", p=128)

    es = contextlib.ExitStack()
    with es:
        sbg = lambda name, shape, dt=F32: es.enter_context(nc.sbuf_tensor(name, shape, dt))
        ident = sbg("identS", [128, 128])
        ident16 = sbg("ident16S", [128, 128], BF16)
        ones_row = sbg("ones_rowS", [1, 128])
        sel2 = sbg("sel2S", [2, 128])
        vpc = sbg("vpcS", [128, 24])
        alpha_bc = sbg("alpha_bcS", [128, 16])
        alpha_s = sbg("alpha_sS", [128, 16])
        bias_pc = sbg("bias_pcS", [128, 16])
        bv_bc = sbg("bv_bcS", [128, D])
        clip2 = sbg("clip2S", [1, 8])
        sig2 = sbg("sig2S", [1, 8])
        c_row = sbg("c_rowS", [1, 16])
        gacc = sbg("gaccS", [128, 1])
        hacc = sbg("haccS", [128, 1])
        gmax_g = sbg("gmax_gS", [1, 8])
        hmax_g = sbg("hmax_gS", [1, 8])
        cc8 = sbg("cc8S", [1, 8])
        sc4 = sbg("sc4S", [1, 4])
        isx_pc = sbg("isx_pcS", [128, 1])
        sx_pc = sbg("sx_pcS", [128, 1])
        is2_pc = sbg("is2_pcS", [128, 1])
        s2_pc = sbg("s2_pcS", [128, 1])
        swg = sbg("swgS", [128, JC])
        swu = sbg("swuS", [128, JC])
        swd = sbg("swdS", [128, DC])

        sem = lambda name: es.enter_context(nc.semaphore(name))
        smSET = sem("smSET"); smWC = sem("smWC")
        smSQ = sem("smSQ"); smVS = sem("smVS"); smRS = sem("smRS")
        smXR = sem("smXR"); smCA = sem("smCA"); smTR = sem("smTR"); smEV = sem("smEV")
        smGM = sem("smGM"); smCC = sem("smCC"); smNQ = sem("smNQ")
        smBC = sem("smBC"); smQV = sem("smQV"); smT2 = sem("smT2"); smE2 = sem("smE2")
        smWT = sem("smWT"); smXN2 = sem("smXN2"); smPM = sem("smPM"); smEB = sem("smEB")
        smKXk = sem("smKXk")
        smAB = sem("smAB"); smKM = sem("smKM"); smBM = sem("smBM")
        smBRa = sem("smBRa"); smBRb = sem("smBRb"); smTRb = sem("smTRb")
        smVL = sem("smVL"); smKL0 = sem("smKL0"); smKL1 = sem("smKL1")
        smSC = sem("smSC"); smEX = sem("smEX"); smAV = sem("smAV")
        smRC = sem("smRC"); smBB = sem("smBB"); smNM = sem("smNM"); smHD = sem("smHD")
        smWQ = sem("smWQ")
        smGM2 = sem("smGM2"); smGE = sem("smGE"); smSCG = sem("smSCG")
        smUM = sem("smUM"); smUE = sem("smUE"); smHP = sem("smHP")
        smHQ = sem("smHQ"); smDM = sem("smDM"); smDE = sem("smDE")
        smXQL = sem("smXQL"); smYM = sem("smYM"); smYE = sem("smYE")
        smYT = sem("smYT"); smYV = sem("smYV")
        smLN2 = sem("smLN2"); smFT = sem("smFT"); smFV = sem("smFV")
        smOUT = sem("smOUT"); smOUTv = sem("smOUTv"); smDBG = sem("smDBG")
        smXAp = [sem("smXAp0"), sem("smXAp1")]
        smSBp = [sem("smSBp0"), sem("smSBp1")]
        smSBvp = [sem("smSBvp0"), sem("smSBvp1")]
        smKXs = sem("smKXs")
        smQLp = [sem("smQLp0"), sem("smQLp1")]
        smWLp = [sem("smWLp0"), sem("smWLp1")]
        smWSp = [sem("smWSp0"), sem("smWSp1")]
        smSGSp = [sem("smSGSp0"), sem("smSGSp1")]
        smSGLp = [sem("smSGLp0"), sem("smSGLp1")]
        smHSqp = [sem("smHSqp0"), sem("smHSqp1")]
        smXLp = [sem("smXLp0"), sem("smXLp1")]
        smOUTp = [sem("smOUTp0"), sem("smOUTp1")]
        smVF = sem("smVF"); smAF = sem("smAF")
        _vfc = [0]; _afc = [0]

        def vsync(v, ins):
            ins.then_inc(smVF, 1)
            _vfc[0] += 1
            v.wait_ge(smVF, _vfc[0])

        def asyncf(s, ins):
            ins.then_inc(smAF, 1)
            _afc[0] += 1
            s.wait_ge(smAF, _afc[0])

        with nc.Block() as blk:
            @blk.sync
            def _(sy):
                pairs = [(ident, ident_d), (ident16, ident16_d), (ones_row, ones_d),
                         (vpc, vec_pc_d), (alpha_bc, alpha_bc_d), (bv_bc, bv_bc_d),
                         (clip2, clip2_d)]
                if "SETUP" in skip:
                    pairs = pairs[:2]

                pairs.append((sel2, sel2_d))
                for dst, srcp in pairs:
                    sy.dma_start(out=dst[:], in_=srcp).then_inc(smSET, 16)
                sy.wait_ge(smSET, 16 * len(pairs))

            @blk.vector
            def _(v):
                v.memset(gacc[:], 0.0)
                v.memset(hacc[:], 0.0)
                v.memset(cc8[:], 0.0)

            @blk.gpsimd
            def _(g):
                if "CAST" in skip:
                    return
                g.dma_start(out=wq16s, in_=Wq).then_inc(smWC, 16)
                g.dma_start(out=wk16s, in_=Wk).then_inc(smWC, 16)
                g.dma_start(out=wv16s, in_=Wv).then_inc(smWC, 16)
                g.dma_start(out=wo16s, in_=Wo).then_inc(smWC, 16)

        esXQ = contextlib.ExitStack()
        es.enter_context(esXQ)
        XQF = [esXQ.enter_context(nc.sbuf_tensor(f"XQF{i}", [128, NQ], BF16)) for i in range(DC)]
        with contextlib.ExitStack() as esAB:
            sbAB = lambda name, shape, dt=F32: esAB.enter_context(nc.sbuf_tensor(name, shape, dt))
            XN = [sbAB(f"XN{i}", [128, NKV], BF16) for i in range(DC)]

            with contextlib.ExitStack() as esA:
                sbA = lambda name, shape, dt=F32: esA.enter_context(nc.sbuf_tensor(name, shape, dt))
                xnq_tok = sbA("xnq_tok", [128, NT_Q * D])
                xa = [sbA(f"xa{i}", [128, D]) for i in range(2)]
                xnf0 = sbA("xnf0", [128, D])
                xnf = [xnf0, xnf0]
                sqscr = sbA("sqscr", [128, D])
                x16 = [sbA(f"x16_{i}", [128, D], BF16) for i in range(2)]
                stt = [sbA(f"stt{i}", [128, 8]) for i in range(2)]
                ptr = [esA.enter_context(nc.psum_tensor(f"ptrA{i}", [128, 128], BF16)) for i in range(4)]
                ptrf = esA.enter_context(nc.psum_tensor("ptrfA", [128, 128], F32))
                pbc = esA.enter_context(nc.psum_tensor("pbcA", [128, 4], F32))

                NTTe = (4 if "TINY" in skip else NT_KV) if "QTILES" in skip else NTT
                with nc.Block() as blk:
                    @blk.sync
                    def _(sy):
                        sy.wait_ge(smSET, 16 * (3 if "SETUP" in skip else 8))
                        for t in range(NTTe):
                            if t >= 2:
                                sy.wait_ge(smXR, t - 1)
                                sy.wait_ge(smSQ, t - 1)
                            src = xkv_t[t] if t < NT_KV else xq_t[t - NT_KV]
                            sy.dma_start(out=xa[t % 2][:], in_=src).then_inc(smXAp[t % 2], 16)
                        if "DUMP" not in skip:
                            sy.wait_ge(smEV, 8 * NTTe)
                            for c in range(DC):
                                sy.dma_start(out=xnqscr[c * 128:(c + 1) * 128, :],
                                             in_=XQF[c][:]).then_inc(smNQ, 16)

                    @blk.scalar
                    def _(s):
                        for t in range(NTTe):
                            s.wait_ge(smXAp[t % 2], 16 * (t // 2 + 1))
                            if t >= 1:
                                s.wait_ge(smVS, t)
                            s.activation(sqscr[:], xa[t % 2][:], AF.Square).then_inc(smSQ, 1)
                            s.wait_ge(smVS, t + 1)
                            asyncf(s, s.activation(stt[t % 2][:, 4:5], stt[t % 2][:, 2:3], AF.Ln))
                            s.activation(stt[t % 2][:, 4:5], stt[t % 2][:, 4:5], AF.Exp,
                                         scale=-0.5).then_inc(smRS, 1)

                    @blk.vector
                    def _(v):
                        for t in range(NTTe):
                            v.wait_ge(smXAp[t % 2], 16 * (t // 2 + 1))
                            st = stt[t % 2]
                            vsync(v, v.tensor_reduce(st[:, 0:1], xa[t % 2][:], axis=AX.X, op=ALU.add))
                            v.tensor_scalar(st[:, 0:1], st[:, 0:1], 1.0 / D, None, ALU.mult)
                            v.wait_ge(smSQ, t + 1)
                            vsync(v, v.tensor_reduce(st[:, 1:2], sqscr[:], axis=AX.X, op=ALU.add))
                            v.tensor_scalar(st[:, 2:3], st[:, 1:2], 1.0 / D, None, ALU.mult)
                            vsync(v, v.tensor_tensor(st[:, 3:4], st[:, 0:1], st[:, 0:1], ALU.mult))
                            vsync(v, v.tensor_tensor(st[:, 2:3], st[:, 2:3], st[:, 3:4], ALU.subtract))
                            v.tensor_scalar(st[:, 2:3], st[:, 2:3], float(EPS), None,
                                            ALU.add).then_inc(smVS, 1)
                            v.tensor_scalar(xnf[t % 2][:], xa[t % 2][:], st[:, 0:1], None,
                                            ALU.subtract).then_inc(smXR, 1)
                            v.wait_ge(smRS, t + 1)
                            v.tensor_scalar(xnf[t % 2][:], xnf[t % 2][:], st[:, 4:5], None, ALU.mult)
                            if t < NT_KV:
                                vsync(v, v.reduce_max(st[:, 5:6], xnf[t % 2][:], axis=AX.X,
                                                      apply_absolute_value=True))
                                v.tensor_max(gacc[:], gacc[:], st[:, 5:6])
                            else:
                                v.tensor_copy(xnq_tok[:, (t - NT_KV) * D:(t - NT_KV + 1) * D], xnf[t % 2][:])
                            if t >= 2:
                                v.wait_ge(smTR, 8 * (t - 1))
                            v.tensor_copy(x16[t % 2][:], xnf[t % 2][:]).then_inc(smCA, 1)
                            if t >= 1:
                                tp = t - 1
                                for c in range(DC):
                                    k = 8 * tp + c
                                    v.wait_ge(smTR, k + 1)
                                    dst = (XN[c][:, tp * 128:(tp + 1) * 128] if tp < NT_KV else
                                           XQF[c][:, (tp - NT_KV) * 128:(tp - NT_KV + 1) * 128])
                                    v.tensor_copy(dst, ptr[k % 4][:]).then_inc(smEV, 1)
                        tp = NTTe - 1
                        for c in range(DC):
                            k = 8 * tp + c
                            v.wait_ge(smTR, k + 1)
                            dstf = (XN[c][:, tp * 128:(tp + 1) * 128] if tp < NT_KV else
                                    XQF[c][:, (tp - NT_KV) * 128:(tp - NT_KV + 1) * 128])
                            v.tensor_copy(dstf, ptr[k % 4][:]).then_inc(smEV, 1)
                        if "GX" in skip:
                            v.memset(gmax_g[:], 1.0)
                            v.tensor_copy(cc8[0:1, 0:1], gmax_g[0:1, 0:1]).then_inc(smGM, 1)
                        else:
                            v.wait_ge(smTR, 8 * NTTe + 1)
                            vsync(v, v.reduce_max(gmax_g[0:1, 0:1], ptrf[0:1, 0:128], axis=AX.X))
                            v.tensor_copy(cc8[0:1, 0:1], gmax_g[0:1, 0:1]).then_inc(smGM, 1)

                    @blk.tensor
                    def _(t_):
                        for t in range(NTTe):
                            t_.wait_ge(smCA, t + 1)
                            for c in range(DC):
                                k = 8 * t + c
                                if k >= 4:
                                    t_.wait_ge(smEV, k - 3)
                                t_.transpose(ptr[k % 4][:], x16[t % 2][:, c * 128:(c + 1) * 128],
                                             ident16[:]).then_inc(smTR, 1)
                        if "GX" not in skip:
                            t_.wait_ge(smEV, 8 * NTTe)
                            t_.transpose(ptrf[0:1, 0:128], gacc[:], ident[:]).then_inc(smTR, 1)

                    @blk.gpsimd
                    def _(g):
                        if "GP" in skip:
                            return
                        g.wait_ge(smGM, 1)
                        g.dma_start(out=ccin1[None, :], in_=cc8[0:1, 0:8]).then_inc(smCC, 16)
                        g.wait_ge(smCC, 16)
                        if "CC" in skip:
                            g.dma_start(out=ccout1, in_=ccin1).then_inc(smCC, 16)
                        else:
                            g.collective_compute("AllReduce", ALU.max, replica_groups=[list(range(8))],
                                                 ins=[ccin1], outs=[ccout1]).then_inc(smCC, 1)
                        g.wait_ge(smCC, 17)
                        g.dma_start(out=gmax_g[0:1, 0:8], in_=ccout1[None, 0:8]).then_inc(smCC, 16)
                        g.wait_ge(smCC, 33)

                qv0 = sbA("qv0", [128, D])
                qv = [qv0, qv0]
                q16 = [sbA(f"q16_{i}", [128, D], BF16) for i in range(2)]

                if "A2B" not in skip:
                    with nc.Block() as blk:
                        @blk.scalar
                        def _(s):
                            if "A2" in skip:
                                return
                            s.activation(sig2[0:1, 0:2], clip2[0:1, 0:2], AF.Sigmoid).then_inc(smGM, 1)

                        @blk.vector
                        def _(v):
                            if "A2" in skip:
                                return
                            v.wait_ge(smGM, 2)
                            vsync(v, v.tensor_tensor(sc4[0:1, 1:2], sig2[0:1, 0:1], gmax_g[0:1, 0:1], ALU.mult))
                            vsync(v, v.tensor_scalar(sc4[0:1, 1:2], sc4[0:1, 1:2], 1.0 / 127.0, None, ALU.mult))
                            v.reciprocal(sc4[0:1, 0:1], sc4[0:1, 1:2]).then_inc(smGM, 1)
                            v.wait_ge(smBC, 1)
                            vsync(v, v.tensor_copy(isx_pc[:], pbc[:, 0:1]))
                            v.tensor_copy(sx_pc[:], pbc[:, 1:2]).then_inc(smGM, 1)
                            for t in range(NT_Q):
                                v.tensor_scalar(qv[t % 2][:], xnq_tok[:, t * D:(t + 1) * D], isx_pc[:, 0:1],
                                                None, ALU.mult)
                                v.tensor_scalar(qv[t % 2][:], qv[t % 2][:], MAGIC, None, ALU.add)
                                v.tensor_scalar(qv[t % 2][:], qv[t % 2][:], MAGIC, None, ALU.subtract)
                                if t >= 2:
                                    v.wait_ge(smT2, 8 * (t - 1))
                                v.tensor_scalar(q16[t % 2][:], qv[t % 2][:], 127.0, -128.0, ALU.min,
                                                ALU.max).then_inc(smQV, 1)
                                if t >= 1:
                                    tp = t - 1
                                    for c in range(DC):
                                        k = 8 * tp + c
                                        v.wait_ge(smT2, k + 1)
                                        v.tensor_copy(XQF[c][:, tp * 128:(tp + 1) * 128],
                                                      ptr[k % 4][:]).then_inc(smE2, 1)
                            tp = NT_Q - 1
                            for c in range(DC):
                                k = 8 * tp + c
                                v.wait_ge(smT2, k + 1)
                                v.tensor_copy(XQF[c][:, tp * 128:(tp + 1) * 128], ptr[k % 4][:]).then_inc(smE2, 1)

                        @blk.tensor
                        def _(t_):
                            if "A2" in skip:
                                return
                            t_.wait_ge(smGM, 3)
                            t_.matmul(pbc[:, 0:2], ones_row[0:1, :], sc4[0:1, 0:2], start=True,
                                      stop=True).then_inc(smBC, 1)
                            t_.wait_ge(smNQ, 16 * DC)
                            for t in range(NT_Q):
                                t_.wait_ge(smQV, t + 1)
                                for c in range(DC):
                                    k = 8 * t + c
                                    if k >= 4:
                                        t_.wait_ge(smE2, k - 3)
                                    t_.transpose(ptr[k % 4][:], q16[t % 2][:, c * 128:(c + 1) * 128],
                                                 ident16[:]).then_inc(smT2, 1)


                if "xn_fm" in dbg or "xq_fm" in dbg or "gscale" in dbg:
                    with nc.Block() as blk:
                        @blk.sync
                        def _(sy):
                            n = 0
                            if "xn_fm" in dbg:
                                dt_ = dbg["xn_fm"].rearrange("(c p) n -> c p n", p=128)
                                for c in range(DC):
                                    sy.dma_start(out=dt_[c], in_=XN[c][:]).then_inc(smDBG, 16)
                                    n += 16
                            if "xq_fm" in dbg:
                                dt_ = dbg["xq_fm"].rearrange("(c p) n -> c p n", p=128)
                                for c in range(DC):
                                    sy.dma_start(out=dt_[c], in_=XQF[c][:]).then_inc(smDBG, 16)
                                    n += 16
                            if "gscale" in dbg:
                                sy.dma_start(out=dbg["gscale"], in_=sc4[0:1, 0:4]).then_inc(smDBG, 16)
                                n += 16
                            sy.wait_ge(smDBG, n)

            if phases >= 2:
                with contextlib.ExitStack() as esB:
                    sbB = lambda name, shape, dt=F32: esB.enter_context(nc.sbuf_tensor(name, shape, dt))
                    WT = [sbB(f"WT{i}", [128, D], BF16) for i in range(DC)]

                    with contextlib.ExitStack() as esB1:
                        sbB1 = lambda name, shape, dt=F32: esB1.enter_context(nc.sbuf_tensor(name, shape, dt))
                        xnqr = [sbB1(f"xnqr{i}", [128, NQ], BF16) for i in range(DC)]
                        Qstg = [sbB1(f"Qstg{i}", [128, NQ], BF16) for i in range(2)]
                        pQ = [esB1.enter_context(nc.psum_tensor(f"pQ{i}", [128, 512], F32)) for i in range(2)]

                        with nc.Block() as blk:
                            @blk.sync
                            def _(sy):
                                sy.wait_ge(smWC, 64)
                                for i in range(DC):
                                    sy.dma_start(out=WT[i][:], in_=wq16s[0:D, i * 128:(i + 1) * 128],
                                                 transpose=True).then_inc(smWT, 16)
                                sy.wait_ge(smNQ, 16 * DC)
                                for i in range(DC):
                                    sy.dma_start(out=xnqr[i][:],
                                                 in_=xnqscr[i * 128:(i + 1) * 128, :]).then_inc(smXN2, 16)
                                if "B1X" in skip or "B1Y" in skip:
                                    return
                                for dc in range(DC):
                                    sy.wait_ge(smEB, 4 * (dc + 1))
                                    r0 = dc * 128
                                    sy.dma_start(out=qscr[r0:r0 + 64, :],
                                                 in_=Qstg[dc % 2][0:64, :]).then_inc(smSBp[dc % 2], 16)
                                    sy.dma_start(out=qscr[r0 + 64:r0 + 128, :],
                                                 in_=Qstg[dc % 2][64:128, :]).then_inc(smSBp[dc % 2], 16)

                            @blk.tensor
                            def _(t_):
                                if "B1X" in skip:
                                    return
                                t_.wait_ge(smWT, 16 * DC)
                                t_.wait_ge(smXN2, 16 * DC)
                                for dc in range(DC):
                                    for nch in range(4):
                                        g = dc * 4 + nch
                                        if g >= 2 and "B1Y" not in skip:
                                            t_.wait_ge(smEB, g - 1)
                                        for i in range(DC):
                                            mm = t_.matmul(pQ[g % 2][:, :], WT[i][:, dc * 128:(dc + 1) * 128],
                                                           xnqr[i][:, nch * 512:(nch + 1) * 512],
                                                           start=(i == 0), stop=(i == DC - 1))
                                        mm.then_inc(smPM, 1)

                            @blk.vector
                            def _(v):
                                if "B1X" in skip or "B1Y" in skip:
                                    return
                                for dc in range(DC):
                                    if dc >= 2:
                                        v.wait_ge(smSBp[dc % 2], 32 * (dc // 2))
                                    for nch in range(4):
                                        g = dc * 4 + nch
                                        v.wait_ge(smPM, g + 1)
                                        v.tensor_scalar(Qstg[dc % 2][:, nch * 512:(nch + 1) * 512],
                                                        pQ[g % 2][:, :], vpc[:, dc:dc + 1], None,
                                                        ALU.add).then_inc(smEB, 1)

                    if "B2" not in skip:
                      with contextlib.ExitStack() as esB2:
                        sbB2 = lambda name, shape, dt=F32: esB2.enter_context(nc.sbuf_tensor(name, shape, dt))
                        vstg = [sbB2(f"vstg{i}", [128, H * 65], BF16) for i in range(2)]
                        pV = [esB2.enter_context(nc.psum_tensor(f"pV{i}", [128, 512], F32)) for i in range(2)]

                        with nc.Block() as blk:
                            @blk.sync
                            def _(sy):
                                sy.wait_ge(smWC, 48)
                                for i in range(DC):
                                    sy.dma_start(out=WT[i][:], in_=wv16s[0:D, i * 128:(i + 1) * 128],
                                                 transpose=True).then_inc(smWT, 16)
                                for mt in range(NT_KV):
                                    sy.wait_ge(smEB, 32 + 3 * (mt + 1))
                                    sy.dma_start(out=vscr[mt * 128:(mt + 1) * 128, :],
                                                 in_=vstg[mt % 2][:]).then_inc(smSBvp[mt % 2], 16)

                            @blk.tensor
                            def _(t_):
                                t_.wait_ge(smWT, 16 * DC * 2)
                                for mt in range(NT_KV):
                                    for nch in range(2):
                                        if mt >= 1:
                                            t_.wait_ge(smEB, 32 + 3 * mt)
                                        for i in range(DC):
                                            mm = t_.matmul(pV[nch][:, :], XN[i][:, mt * 128:(mt + 1) * 128],
                                                           WT[i][:, nch * 512:(nch + 1) * 512],
                                                           start=(i == 0), stop=(i == DC - 1))
                                        mm.then_inc(smPM, 1)

                            @blk.vector
                            def _(v):
                                for mt in range(NT_KV):
                                    if mt >= 2:
                                        v.wait_ge(smSBvp[mt % 2], 16 * (mt // 2))
                                    vt = vstg[mt % 2][:].rearrange("p (h c) -> p h c", c=65)
                                    v.memset(vt[:, :, 64:65], 1.0).then_inc(smEB, 1)
                                    for nch in range(2):
                                        v.wait_ge(smPM, 32 + mt * 2 + nch + 1)
                                        src = pV[nch][:, :].rearrange("p (h c) -> p h c", c=64)
                                        bvs = bv_bc[:, nch * 512:(nch + 1) * 512].rearrange(
                                            "p (h c) -> p h c", c=64)
                                        v.tensor_tensor(vt[:, nch * 8:(nch + 1) * 8, 0:64], src, bvs,
                                                        ALU.add).then_inc(smEB, 1)

                    if "B3" not in skip:
                      with contextlib.ExitStack() as esB3:
                        sbB3 = lambda name, shape, dt=F32: esB3.enter_context(nc.sbuf_tensor(name, shape, dt))
                        Kstg = [sbB3(f"Kstg{i}", [128, NKV], BF16) for i in range(2)]
                        kmx = sbB3("kmx", [128, 1])
                        pK = [esB3.enter_context(nc.psum_tensor(f"pK{i}", [128, 512], F32)) for i in range(2)]

                        with nc.Block() as blk:
                            @blk.sync
                            def _(sy):
                                sy.wait_ge(smWC, 32)
                                for i in range(DC):
                                    sy.dma_start(out=WT[i][:], in_=wk16s[0:D, i * 128:(i + 1) * 128],
                                                 transpose=True).then_inc(smWT, 16)
                                for dc in range(DC):
                                    sy.wait_ge(smEB, 128 + 8 * (dc + 1))
                                    sy.dma_start(out=kscr[dc * 128:(dc + 1) * 128, :],
                                                 in_=Kstg[dc % 2][:]).then_inc(smSBp[dc % 2], 16)
                                    sy.wait_ge(smKXk, dc + 1)
                                    sy.dma_start(out=kmaxscr[dc:dc + 1, :], in_=kmx[:, 0:1]).then_inc(smKXs, 16)

                            @blk.tensor
                            def _(t_):
                                t_.wait_ge(smWT, 16 * DC * 3)
                                for dc in range(DC):
                                    for mch in range(8):
                                        g = dc * 8 + mch
                                        if g >= 2:
                                            t_.wait_ge(smEB, 128 + g - 1)
                                        for i in range(DC):
                                            mm = t_.matmul(pK[g % 2][:, :], WT[i][:, dc * 128:(dc + 1) * 128],
                                                           XN[i][:, mch * 512:(mch + 1) * 512],
                                                           start=(i == 0), stop=(i == DC - 1))
                                        mm.then_inc(smPM, 1)

                            @blk.vector
                            def _(v):
                                for dc in range(DC):
                                    if dc >= 2:
                                        v.wait_ge(smSBp[dc % 2], 128 + 16 * (dc // 2))
                                    for mch in range(8):
                                        g = dc * 8 + mch
                                        v.wait_ge(smPM, 96 + g + 1)
                                        v.tensor_scalar(Kstg[dc % 2][:, mch * 512:(mch + 1) * 512],
                                                        pK[g % 2][:, :], vpc[:, 8 + dc:9 + dc], None,
                                                        ALU.add).then_inc(smEB, 1)
                                    if dc >= 1:
                                        v.wait_ge(smKXs, 16 * dc)
                                    v.reduce_max(kmx[:], Kstg[dc % 2][:], axis=AX.X,
                                                 apply_absolute_value=True).then_inc(smKXk, 1)

                    if "B4" not in skip:
                      with contextlib.ExitStack() as esB4:
                        sbB4 = lambda name, shape, dt=F32: esB4.enter_context(nc.sbuf_tensor(name, shape, dt))
                        qb = [sbB4(f"qb{i}", [64, NQ], BF16) for i in range(2)]
                        absq = [sbB4(f"absq{i}", [64, NQ], BF16) for i in range(2)]
                        kmlf = [sbB4(f"kmlf{i}", [64, 1]) for i in range(2)]
                        kml16 = [sbB4(f"kml16{i}", [64, 1], BF16) for i in range(2)]
                        bmax = [sbB4(f"bmax{i}", [128, 1]) for i in range(2)]
                        absal = sbB4("absal", [128, 16])
                        pB = esB4.enter_context(nc.psum_tensor("pB", [128, 16], F32))
                        pT1 = esB4.enter_context(nc.psum_tensor("pT1", [128, 128], F32))
                        pC16 = esB4.enter_context(nc.psum_tensor("pC16", [128, 16], F32))

                        with nc.Block() as blk:
                            @blk.sync
                            def _(sy):
                                sy.wait_ge(smSBp[0], 32 * (DC // 2))
                                sy.wait_ge(smSBp[1], 32 * (DC // 2))
                                sy.wait_ge(smKXs, 16 * DC)
                                for h in range(H):
                                    if h >= 2:
                                        sy.wait_ge(smAB, h - 1)
                                        sy.wait_ge(smKM, h - 1)
                                    sy.dma_start(out=qb[h % 2][:],
                                                 in_=qscr[h * 64:(h + 1) * 64, :]).then_inc(smQLp[h % 2], 16)
                                    sy.dma_start(out=kmlf[h % 2][:],
                                                 in_=kmaxscr[h // 2:h // 2 + 1,
                                                             (h % 2) * 64:(h % 2) * 64 + 64]).then_inc(smQLp[h % 2], 16)

                            @blk.scalar
                            def _(s):
                                for h in range(H):
                                    s.wait_ge(smQLp[h % 2], 32 * (h // 2 + 1))
                                    s.activation(absq[h % 2][:], qb[h % 2][:], AF.Abs).then_inc(smAB, 1)
                                s.activation(absal[:], alpha_bc[:], AF.Abs).then_inc(smAB, 1)

                            @blk.vector
                            def _(v):
                                for h in range(H):
                                    v.wait_ge(smQLp[h % 2], 32 * (h // 2 + 1))
                                    v.tensor_copy(kml16[h % 2][:], kmlf[h % 2][:]).then_inc(smKM, 1)
                                    v.wait_ge(smBM, h + 1)
                                    v.reduce_max(bmax[h % 2][:], pB[:, 0:16], axis=AX.X).then_inc(smBRa, 1)
                                    v.wait_ge(smTRb, h + 1)
                                    v.reduce_max(c_row[0:1, h:h + 1], pT1[0:1, 0:128],
                                                 axis=AX.X).then_inc(smBRb, 1)
                                v.wait_ge(smBC, 2)
                                v.wait_ge(smAB, H + 1)
                                vsync(v, v.tensor_tensor(bias_pc[:], pC16[:, 0:16], absal[:], ALU.mult))
                                v.tensor_scalar(bias_pc[:], bias_pc[:], -0.0625, None, ALU.mult)
                                v.tensor_scalar(alpha_s[:], alpha_bc[:], 0.125, None,
                                                ALU.mult).then_inc(smBRb, 1)

                            @blk.tensor
                            def _(t_):
                                for h in range(H):
                                    t_.wait_ge(smAB, h + 1)
                                    t_.wait_ge(smKM, h + 1)
                                    if h >= 1:
                                        t_.wait_ge(smBRa, h)
                                    for nt in range(16):
                                        mm = t_.matmul(pB[:, nt:nt + 1],
                                                       absq[h % 2][:, nt * 128:(nt + 1) * 128],
                                                       kml16[h % 2][:], start=True, stop=True)
                                    mm.then_inc(smBM, 1)
                                    t_.wait_ge(smBRa, h + 1)
                                    if h >= 1:
                                        t_.wait_ge(smBRb, h)
                                    t_.transpose(pT1[0:1, 0:128], bmax[h % 2][:], ident[:]).then_inc(smTRb, 1)
                                t_.wait_ge(smBRb, H)
                                t_.matmul(pC16[:, 0:16], ones_row[0:1, :], c_row[0:1, :], start=True,
                                          stop=True).then_inc(smBC, 1)

        qp_uid = [0]

        def quant_pass(wsrc, dst_scr, swt, nchunks, C, sems, lb, qb):
            smLp, smQ, smSp = sems
            qp_uid[0] += 1
            u = qp_uid[0]
            with contextlib.ExitStack() as esq:
                sbq = lambda name, shape, dt=F32: esq.enter_context(nc.sbuf_tensor(name, shape, dt))
                wch = [sbq(f"wch{i}_{u}", [128, C]) for i in range(2)]
                vqs = sbq(f"vqs{u}", [128, C])
                w16o = [sbq(f"w16o{i}_{u}", [128, C], BF16) for i in range(2)]
                qs = sbq(f"qs{u}", [128, 4])

                with nc.Block() as blk:
                    @blk.sync
                    def _(sy):
                        for j in range(nchunks + 2):
                            if 2 <= j:
                                sy.wait_ge(smQ, qb + j - 1)
                            if j < nchunks:
                                sy.dma_start(out=wch[j % 2][:],
                                             in_=wsrc[j * 128:(j + 1) * 128, :]).then_inc(smLp[j % 2], 16)
                            if j >= 2:
                                js = j - 2
                                sy.dma_start(out=dst_scr[js * 128:(js + 1) * 128, :],
                                             in_=w16o[js % 2][:]).then_inc(smSp[js % 2], 16)

                    @blk.vector
                    def _(v):
                        for j in range(nchunks):
                            v.wait_ge(smLp[j % 2], lb[j % 2] + 16 * (j // 2 + 1))
                            vsync(v, v.reduce_max(qs[:, 0:1], wch[j % 2][:], axis=AX.X,
                                                  apply_absolute_value=True))
                            vsync(v, v.reciprocal(qs[:, 1:2], qs[:, 0:1]))
                            vsync(v, v.tensor_scalar(qs[:, 1:2], qs[:, 1:2], 127.0, None, ALU.mult))
                            v.tensor_scalar(swt[:, j:j + 1], qs[:, 0:1], 1.0 / 127.0, None, ALU.mult)
                            v.tensor_scalar(vqs[:], wch[j % 2][:], qs[:, 1:2], None, ALU.mult)
                            v.tensor_scalar(vqs[:], vqs[:], MAGIC, None, ALU.add)
                            v.tensor_scalar(vqs[:], vqs[:], MAGIC, None, ALU.subtract)
                            if j >= 2:
                                v.wait_ge(smSp[j % 2], lb[j % 2] + 16 * (j // 2))
                            v.tensor_copy(w16o[j % 2][:], vqs[:]).then_inc(smQ, 1)

        fuse_gate = phases >= 4
        if fuse_gate:
            quant_pass(wg_d, wgq_scr, swg, JC, D, (smWLp, smWQ, smWSp), [0, 0], 0)
        WT_C = 16 * DC * 3 + (16 * 8 * JC if fuse_gate else 0)

        if phases >= 3:
            esCE = es.enter_context(contextlib.ExitStack())
            ATT = [esCE.enter_context(nc.sbuf_tensor(f"ATT{i}", [128, NQ], BF16)) for i in range(DC)]

            with contextlib.ExitStack() as esC:
                sbC = lambda name, shape, dt=F32: esC.enter_context(nc.sbuf_tensor(name, shape, dt))
                VAUG = [sbC(f"VAUG{i}", [128, H * 65], BF16) for i in range(NT_KV)]
                Khd = [sbC(f"Khd{i}", [64, NKV], BF16) for i in range(2)]
                Qhd = [sbC(f"Qhd{i}", [64, NQ], BF16) for i in range(2)]
                ptile = [sbC(f"ptile{i}", [128, 1024], BF16) for i in range(3)]
                dnst = [sbC(f"dnst{i}", [1, 1024]) for i in range(2)]
                WSG = [sbC(f"WSG{i}", [128, 1024], BF16) for i in range(2)]
                sgstg = [sbC(f"sgstg{i}", [128, NQ]) for i in range(2)]
                Sp = [esC.enter_context(nc.psum_tensor(f"Sp{i}", [128, 1024], F32)) for i in range(2)]
                AVp = esC.enter_context(nc.psum_tensor("AVp", [65, 1024], F32))
                pG = [esC.enter_context(nc.psum_tensor(f"pGc{i}", [128, 512], F32)) for i in range(2)]
                smKLh = [smKL0, smKL1]
                gate_slots = {}
                if fuse_gate:
                    gq = 0
                    for gi in range(H * 2 * 32):
                        if gi % 11 == 5 and gq < 4 * JC:
                            gate_slots[gi] = gq
                            gq += 1
                    assert gq == 4 * JC, gq

                with nc.Block() as blk:
                    @blk.sync
                    def _(sy):
                        sy.wait_ge(smSBvp[0], 16 * (NT_KV // 2))
                        sy.wait_ge(smSBvp[1], 16 * (NT_KV // 2))
                        sy.wait_ge(smSBp[0], 192)
                        sy.wait_ge(smSBp[1], 192)
                        for h in range(2):
                            r0 = h * 64
                            sy.dma_start(out=Khd[h % 2][:], in_=kscr[r0:r0 + 64, :]).then_inc(smKLh[h % 2], 16)
                            sy.dma_start(out=Qhd[h % 2][:], in_=qscr[r0:r0 + 64, :]).then_inc(smKLh[h % 2], 16)
                        for mt in range(NT_KV):
                            sy.dma_start(out=VAUG[mt][:],
                                         in_=vscr[mt * 128:(mt + 1) * 128, :]).then_inc(smVL, 16)
                        if fuse_gate:
                            sy.wait_ge(smWSp[0], 176)
                            sy.wait_ge(smWSp[1], 176)
                            for j in range(JC + 2):
                                if j < JC:
                                    if j >= 2:
                                        sy.wait_ge(smGM2, 4 * (j - 1))
                                    for i in range(DC):
                                        sy.dma_start(
                                            out=WSG[j % 2][:, i * 128:(i + 1) * 128],
                                            in_=wgq_scr[j * 128:(j + 1) * 128, i * 128:(i + 1) * 128],
                                            transpose=True).then_inc(smWT, 16)
                                if j >= 2:
                                    js = j - 2
                                    sy.wait_ge(smGE, 4 * (js + 1))
                                    sy.dma_start(out=sgscr[js * 128:(js + 1) * 128, :],
                                                 in_=sgstg[js % 2][:]).then_inc(smSGSp[js % 2], 16)

                    @blk.tensor
                    def _(t_):
                        t_.wait_ge(smVL, 16 * NT_KV)

                        def gate_group(gq):
                            j, nch = gq // 4, gq % 4
                            if nch == 0:
                                t_.wait_ge(smWT, 16 * DC * 3 + 16 * 8 * (j + 1))
                            if gq >= 2:
                                t_.wait_ge(smGE, gq - 1)
                            for i in range(DC):
                                mm = t_.matmul(pG[gq % 2][:, :], WSG[j % 2][:, i * 128:(i + 1) * 128],
                                               XQF[i][:, nch * 512:(nch + 1) * 512],
                                               start=(i == 0), stop=(i == DC - 1))
                            mm.then_inc(smGM2, 1)

                        def av(kav):
                            t_.wait_ge(smEX, kav + 1)
                            mloc = kav % 32
                            if mloc == 0:
                                t_.wait_ge(smNM, kav // 32)
                            hh = kav // 64
                            for half in range(2):
                                mm = t_.matmul(AVp[0:65, half * 512:(half + 1) * 512],
                                               VAUG[mloc][:, hh * 65:(hh + 1) * 65],
                                               ptile[kav % 3][:, half * 512:(half + 1) * 512],
                                               start=(mloc == 0), stop=(mloc == 31))
                            mm.then_inc(smAV, 1)

                        for h in range(H):
                            t_.wait_ge(smKLh[h % 2], 32 * (h // 2 + 1))
                            for nf in range(2):
                                it = h * 2 + nf

                                def scores(m, h=h, nf=nf, it=it):
                                    k = it * 32 + m
                                    if k >= 2:
                                        t_.wait_ge(smEX, k - 1)
                                    for half in range(2):
                                        mm = t_.matmul(Sp[k % 2][:, half * 512:(half + 1) * 512],
                                                       Khd[h % 2][:, m * 128:(m + 1) * 128],
                                                       Qhd[h % 2][:, nf * 1024 + half * 512:
                                                                   nf * 1024 + (half + 1) * 512],
                                                       start=True, stop=True)
                                    mm.then_inc(smSC, 1)

                                scores(0)
                                for m in range(32):
                                    if m < 31:
                                        scores(m + 1)
                                    k = it * 32 + m
                                    if k >= 1:
                                        av(k - 1)
                                    if k in gate_slots:
                                        gate_group(gate_slots[k])
                            t_.nop().then_inc(smHD, 1)
                        av(H * 2 * 32 - 1)

                    @blk.scalar
                    def _(s):
                        for it in range(H * 2):
                            h = it // 2
                            for m in range(32):
                                k = it * 32 + m
                                s.wait_ge(smSC, k + 1)
                                if k >= 3:
                                    s.wait_ge(smAV, k - 2)
                                s.activation(ptile[k % 3][:], Sp[k % 2][:, :], AF.Exp,
                                             bias=bias_pc[:, h:h + 1],
                                             scale=alpha_s[:, h:h + 1]).then_inc(smEX, 1)

                    @blk.vector
                    def _(v):
                        for it in range(H * 2):
                            h, nf = it // 2, it % 2
                            for gi in range(it * 32, (it + 1) * 32):
                                if gi in gate_slots:
                                    gq = gate_slots[gi]
                                    j, nch = gq // 4, gq % 4
                                    v.wait_ge(smGM2, gq + 1)
                                    if nch == 0 and j >= 2:
                                        v.wait_ge(smSGSp[j % 2], 16 * (j // 2))
                                    v.tensor_copy(sgstg[j % 2][:, nch * 512:(nch + 1) * 512],
                                                  pG[gq % 2][:, :]).then_inc(smGE, 1)
                            v.wait_ge(smAV, 32 * (it + 1))
                            c = h // 2
                            rows = slice((h % 2) * 64, (h % 2) * 64 + 64)
                            v.tensor_copy(ATT[c][rows, nf * 1024:(nf + 1) * 1024], AVp[0:64, :])
                            if it >= 2:
                                v.wait_ge(smBB, 16 * (it - 1))
                            v.tensor_copy(dnst[it % 2][:], AVp[64:65, 0:1024]).then_inc(smNM, 1)

                    @blk.gpsimd
                    def _(g):
                        for h in range(H):
                            for nf in range(2):
                                it = h * 2 + nf
                                g.wait_ge(smNM, it + 1)
                                g.dma_start(out=dscr[h:h + 1, nf * 1024:(nf + 1) * 1024],
                                            in_=dnst[it % 2][:]).then_inc(smBB, 16)
                            if h + 2 < H:
                                g.wait_ge(smHD, h + 1)
                                r0 = (h + 2) * 64
                                g.dma_start(out=Khd[h % 2][:],
                                            in_=kscr[r0:r0 + 64, :]).then_inc(smKLh[h % 2], 16)
                                g.dma_start(out=Qhd[h % 2][:],
                                            in_=qscr[r0:r0 + 64, :]).then_inc(smKLh[h % 2], 16)

        dstop = 99
        for s_ in skip:
            if s_.startswith("DSTOP"):
                dstop = int(s_[5:])
        if phases >= 4:
            esDE = es.enter_context(contextlib.ExitStack())
            FFN = [esDE.enter_context(nc.sbuf_tensor(f"FFN{i}", [128, NQ], BF16)) for i in range(DC)]

            with contextlib.ExitStack() as esD0:


                if dstop >= 3:
                    quant_pass(wu_d, wuq_scr, swu, JC, D, (smWLp, smWQ, smWSp), [176, 176], JC)

                if dstop >= 4:
                  with contextlib.ExitStack() as esD2:
                      sbD2 = lambda name, shape, dt=F32: esD2.enter_context(nc.sbuf_tensor(name, shape, dt))
                      WSTu = [sbD2(f"WSTu{i}", [128, II], BF16) for i in range(DC)]
                      upstg = [sbD2(f"upstg{i}", [128, NQ]) for i in range(2)]
                      sgld = [sbD2(f"sgld{i}", [128, NQ]) for i in range(2)]
                      h16s = [sbD2(f"h16s{i}", [128, NQ], BF16) for i in range(2)]
                      SCU = sbD2("SCU", [128, JC])
                      SCG2 = sbD2("SCG2", [128, JC])
                      sgsil = [sbD2(f"sgsil{i}", [128, NQ]) for i in range(2)]
                      hstat = sbD2("hstat", [128, 2])
                      pU = [esD2.enter_context(nc.psum_tensor(f"pU{i}", [128, 512], F32)) for i in range(2)]

                      with nc.Block() as blk:
                          @blk.sync
                          def _(sy):
                              sy.wait_ge(smWSp[0], 352)
                              sy.wait_ge(smWSp[1], 352)
                              for i in range(DC):
                                  sy.dma_start(out=WSTu[i][:], in_=wuq_scr[0:II, i * 128:(i + 1) * 128],
                                               transpose=True).then_inc(smWT, 16)
                              for j in range(JC + 2):
                                  if 2 <= j:
                                      sy.wait_ge(smHP, j - 1)
                                  if j < JC:
                                      sy.dma_start(out=sgld[j % 2][:],
                                                   in_=sgscr[j * 128:(j + 1) * 128, :]).then_inc(smSGLp[j % 2], 16)
                                  if j >= 2:
                                      js = j - 2
                                      sy.dma_start(out=hscr[js * 128:(js + 1) * 128, :],
                                                   in_=h16s[js % 2][:]).then_inc(smSGSp[js % 2], 16)

                          @blk.scalar
                          def _(s):
                              s.wait_ge(smSCG, 1)
                              for j in range(JC):
                                  s.wait_ge(smSGLp[j % 2], 16 * (j // 2 + 1))
                                  if j >= 2:
                                      s.wait_ge(smHP, j - 1)
                                  s.activation(sgsil[j % 2][:], sgld[j % 2][:], AF.Silu,
                                               scale=SCG2[:, j:j + 1]).then_inc(smAB, 1)

                          @blk.tensor
                          def _(t_):
                              t_.wait_ge(smWT, WT_C + 16 * DC)
                              for j in range(JC):
                                  for nch in range(4):
                                      g = j * 4 + nch
                                      if g >= 2:
                                          t_.wait_ge(smUE, g - 1)
                                      for i in range(DC):
                                          mm = t_.matmul(pU[g % 2][:, :], WSTu[i][:, j * 128:(j + 1) * 128],
                                                         XQF[i][:, nch * 512:(nch + 1) * 512],
                                                         start=(i == 0), stop=(i == DC - 1))
                                      mm.then_inc(smUM, 1)

                          @blk.vector
                          def _(v):
                              vsync(v, v.tensor_scalar(SCU[:], swu[:], sx_pc[:, 0:1], None, ALU.mult))
                              v.tensor_scalar(SCG2[:], swg[:], sx_pc[:, 0:1], None, ALU.mult).then_inc(smSCG, 1)
                              for j in range(JC):
                                  for nch in range(4):
                                      g = j * 4 + nch
                                      v.wait_ge(smUM, g + 1)
                                      v.tensor_scalar(upstg[j % 2][:, nch * 512:(nch + 1) * 512],
                                                      pU[g % 2][:, :], SCU[:, j:j + 1], None,
                                                      ALU.mult).then_inc(smUE, 1)
                                  v.wait_ge(smAB, H + 1 + j + 1)
                                  if j >= 2:
                                      v.wait_ge(smSGSp[j % 2], 176 + 16 * (j // 2))
                                  v.tensor_tensor(h16s[j % 2][:], sgsil[j % 2][:], upstg[j % 2][:], ALU.mult)
                                  vsync(v, v.reduce_max(hstat[:, 0:1], h16s[j % 2][:], axis=AX.X,
                                                        apply_absolute_value=True))
                                  v.tensor_max(hacc[:], hacc[:], hstat[:, 0:1]).then_inc(smHP, 1)

                if dstop >= 6:
                    quant_pass(wd_d, wdq_scr, swd, DC, II, (smWLp, smWQ, smWSp), [352, 352], 2 * JC)

                if dstop >= 5:
                  with contextlib.ExitStack() as esD3:
                      ptrD = esD3.enter_context(nc.psum_tensor("ptrD", [128, 128], F32))
                      pbcD = esD3.enter_context(nc.psum_tensor("pbcD", [128, 4], F32))

                      with nc.Block() as blk:
                          @blk.tensor
                          def _(t_):
                              t_.transpose(ptrD[0:1, 0:128], hacc[:], ident[:]).then_inc(smGM, 1)
                              t_.wait_ge(smGM, 7)
                              t_.matmul(pbcD[:, 0:2], ones_row[0:1, :], sc4[0:1, 2:4], start=True,
                                        stop=True).then_inc(smBC, 1)

                          @blk.vector
                          def _(v):
                              v.wait_ge(smGM, 5)
                              vsync(v, v.reduce_max(hmax_g[0:1, 0:1], ptrD[0:1, 0:128], axis=AX.X))
                              v.tensor_copy(cc8[0:1, 0:1], hmax_g[0:1, 0:1]).then_inc(smGM, 1)
                              v.wait_ge(smCC, 66)
                              vsync(v, v.tensor_tensor(sc4[0:1, 3:4], sig2[0:1, 1:2], hmax_g[0:1, 0:1], ALU.mult))
                              vsync(v, v.tensor_scalar(sc4[0:1, 3:4], sc4[0:1, 3:4], 1.0 / 127.0, None, ALU.mult))
                              v.reciprocal(sc4[0:1, 2:3], sc4[0:1, 3:4]).then_inc(smGM, 1)
                              v.wait_ge(smBC, 3)
                              vsync(v, v.tensor_copy(is2_pc[:], pbcD[:, 0:1]))
                              v.tensor_copy(s2_pc[:], pbcD[:, 1:2]).then_inc(smGM, 1)

                          @blk.gpsimd
                          def _(g):
                              g.wait_ge(smGM, 6)
                              g.dma_start(out=ccin2[None, :], in_=cc8[0:1, 0:8]).then_inc(smCC, 16)
                              g.wait_ge(smCC, 49)
                              g.collective_compute("AllReduce", ALU.max, replica_groups=[list(range(8))],
                                                   ins=[ccin2], outs=[ccout2]).then_inc(smCC, 1)
                              g.wait_ge(smCC, 50)
                              g.dma_start(out=hmax_g[0:1, 0:8], in_=ccout2[None, 0:8]).then_inc(smCC, 16)
                              g.wait_ge(smCC, 66)

                if dstop >= 6:
                  with contextlib.ExitStack() as esDq:
                      sbDq = lambda name, shape, dt=F32: esDq.enter_context(nc.sbuf_tensor(name, shape, dt))
                      hld = [sbDq(f"hld{i}", [128, NQ], BF16) for i in range(2)]
                      hvq = sbDq("hvq", [128, NQ])
                      hqo = [sbDq(f"hqo{i}", [128, NQ], BF16) for i in range(2)]

                      with nc.Block() as blk:
                          @blk.sync
                          def _(sy):
                              sy.wait_ge(smSGSp[0], 352)
                              sy.wait_ge(smSGSp[1], 352)
                              for j in range(JC + 2):
                                  if 2 <= j:
                                      sy.wait_ge(smHQ, j - 1)
                                  if j < JC:
                                      sy.dma_start(out=hld[j % 2][:],
                                                   in_=hscr[j * 128:(j + 1) * 128, :]).then_inc(smSGLp[j % 2], 16)
                                  if j >= 2:
                                      js = j - 2
                                      sy.dma_start(out=hqscr[js * 128:(js + 1) * 128, :],
                                                   in_=hqo[js % 2][:]).then_inc(smHSqp[js % 2], 16)

                          @blk.vector
                          def _(v):
                              for j in range(JC):
                                  v.wait_ge(smSGLp[j % 2], 176 + 16 * (j // 2 + 1))
                                  v.tensor_scalar(hvq[:], hld[j % 2][:], is2_pc[:, 0:1], None, ALU.mult)
                                  v.tensor_scalar(hvq[:], hvq[:], MAGIC, None, ALU.add)
                                  v.tensor_scalar(hvq[:], hvq[:], MAGIC, None, ALU.subtract)
                                  if j >= 2:
                                      v.wait_ge(smHSqp[j % 2], 16 * (j // 2))
                                  v.tensor_scalar(hqo[j % 2][:], hvq[:], 127.0, -128.0, ALU.min,
                                                  ALU.max).then_inc(smHQ, 1)

                if dstop >= 7:
                  with contextlib.ExitStack() as esD4:
                      sbD4 = lambda name, shape, dt=F32: esD4.enter_context(nc.sbuf_tensor(name, shape, dt))
                      WDT = [sbD4(f"WDT{i}", [128, D], BF16) for i in range(JC)]
                      hq2 = [sbD4(f"hq2_{i}", [128, NQ], BF16) for i in range(2)]
                      SCD = sbD4("SCD", [128, DC])
                      pD2 = [esD4.enter_context(nc.psum_tensor(f"pD2_{i}", [128, 512], F32))
                             for i in range(8)]

                      with nc.Block() as blk:
                          @blk.sync
                          def _(sy):
                              sy.wait_ge(smWSp[0], 416)
                              sy.wait_ge(smWSp[1], 416)
                              if "D4Z" not in skip:
                                  for j in range(JC):
                                      sy.dma_start(out=WDT[j][:], in_=wdq_scr[0:D, j * 128:(j + 1) * 128],
                                                   transpose=True).then_inc(smWT, 16)
                              sy.wait_ge(smHSqp[0], 16 * 11)
                              sy.wait_ge(smHSqp[1], 16 * 11)
                              if "D4W" in skip:
                                  return
                              for hcg in range(4):
                                  if "D4X" in skip and hcg > 0:
                                      break
                                  for j in range(JC):
                                      kk = hcg * JC + j
                                      if kk >= 2 and "D4X" not in skip:
                                          sy.wait_ge(smDM, kk - 1)
                                      sy.dma_start(out=hq2[kk % 2][:],
                                                   in_=hqscr[j * 128:(j + 1) * 128, :]).then_inc(smSGLp[kk % 2], 16)

                          @blk.tensor
                          def _(t_):
                              t_.wait_ge(smWT, 16 * DC * 5 + 16 * JC)
                              for hcg in range(4):
                                  if hcg >= 1:
                                      t_.wait_ge(smDE, 8 * hcg)
                                  for j in range(JC):
                                      kk = hcg * JC + j
                                      t_.wait_ge(smSGLp[kk % 2], 352 + 16 * (kk // 2 + 1))
                                      for hc2 in range(2):
                                          hc = hcg * 2 + hc2
                                          for nch in range(4):
                                              mm = t_.matmul(pD2[hc2 * 4 + nch][:, :],
                                                             WDT[j][:, hc * 128:(hc + 1) * 128],
                                                             hq2[kk % 2][:, nch * 512:(nch + 1) * 512],
                                                             start=(j == 0), stop=(j == JC - 1))
                                      mm.then_inc(smDM, 1)

                          @blk.vector
                          def _(v):
                              if "D4X" in skip or "D4Y" in skip:
                                  return
                              vsync(v, v.tensor_scalar(SCD[:], swd[:], s2_pc[:, 0:1], None, ALU.mult))
                              for hcg in range(4):
                                  v.wait_ge(smDM, JC * (hcg + 1))
                                  for hc2 in range(2):
                                      hc = hcg * 2 + hc2
                                      for nch in range(4):
                                          v.tensor_scalar(FFN[hc][:, nch * 512:(nch + 1) * 512],
                                                          pD2[hc2 * 4 + nch][:, :], SCD[:, hc:hc + 1],
                                                          None, ALU.mult).then_inc(smDE, 1)

        if phases >= 5:
            with contextlib.ExitStack() as esE:
                sbE = lambda name, shape, dt=F32: esE.enter_context(nc.sbuf_tensor(name, shape, dt))
                WOT = [sbE(f"WOT{i}", [128, D], BF16) for i in range(DC)]
                gm_bc = sbE("gm_bcS", [128, D])
                bt_bc = sbE("bt_bcS", [128, D])
                XNQR = XQF
                yfm = [sbE(f"yfm{i}", [128, 512]) for i in range(DC)]
                ytok = [sbE(f"ytok{i}", [128, D]) for i in range(4)]
                sqE = sbE("sqE", [128, D])
                stE = sbE("stE", [128, 8])
                yln = sbE("yln", [128, D])
                attnb = sbE("attnb", [128, D])
                fftok = sbE("fftok", [128, D], BF16)
                xts = [sbE(f"xts{i}", [128, D]) for i in range(2)]
                outstg = [sbE(f"outstg{i}", [128, D]) for i in range(2)]
                drec = sbE("drec", [128, 256])
                drow2 = sbE("drow2", [2, NQ])
                recn = sbE("recn", [128, 512])
                pBC2 = esE.enter_context(nc.psum_tensor("pBC2", [128, 512], F32))
                pY = [esE.enter_context(nc.psum_tensor(f"pY{i}", [128, 512], F32)) for i in range(2)]
                ptE = [esE.enter_context(nc.psum_tensor(f"ptE{i}", [128, 128], F32)) for i in range(2)]
                ptF = [esE.enter_context(nc.psum_tensor(f"ptF{i}", [128, 128], BF16)) for i in range(2)]

                with nc.Block() as blk:
                    @blk.sync
                    def _(sy):
                        sy.wait_ge(smWC, 64)
                        sy.dma_start(out=gm_bc[:], in_=gm_bc_d).then_inc(smSET, 16)
                        sy.dma_start(out=bt_bc[:], in_=bt_bc_d).then_inc(smSET, 16)
                        if "ENON" not in skip:
                            sy.wait_ge(smBB, 16 * H * 2)
                            sy.dma_start(out=drec[:], in_=dscr.rearrange("h n -> (h n)").rearrange(
                                "(p f) -> p f", p=128)).then_inc(smRC, 16)
                            sy.wait_ge(smRC, 17)
                            sy.dma_start(out=dscr.rearrange("h n -> (h n)").rearrange(
                                "(p f) -> p f", p=128), in_=drec[:]).then_inc(smRC, 16)
                            for c in range(DC):
                                sy.wait_ge(smRC, 33)
                                if c >= 1:
                                    sy.wait_ge(smNM, H * 2 + 4 * c)
                                sy.dma_start(out=drow2[:], in_=dscr[2 * c:2 * c + 2, :]).then_inc(smBB, 16)
                        for i in range(DC):
                            sy.dma_start(out=WOT[i][:], in_=wo16s[0:D, i * 128:(i + 1) * 128],
                                         transpose=True).then_inc(smWT, 16)
                        for i in range(DC):
                            sy.dma_start(out=XNQR[i][:],
                                         in_=xnqscr[i * 128:(i + 1) * 128, :]).then_inc(smXQL, 16)
                        for nt in range(NT_Q + 2):
                            if 2 <= nt:
                                sy.wait_ge(smLN2, nt - 1)
                            if nt < NT_Q:
                                sy.dma_start(out=xts[nt % 2][:], in_=xq_t[nt]).then_inc(smXLp[nt % 2], 16)
                            if nt >= 2:
                                ns = nt - 2
                                sy.wait_ge(smOUTv, ns + 1)
                                sy.dma_start(out=out_t[ns], in_=outstg[ns % 2][:]).then_inc(smOUTp[ns % 2], 16)
                        sy.wait_ge(smOUTp[0], 16 * (NT_Q // 2))
                        sy.wait_ge(smOUTp[1], 16 * (NT_Q // 2))

                    @blk.tensor
                    def _(t_):
                        if "ENON" not in skip:
                            for c in range(DC):
                                for nch in range(4):
                                    kb = c * 4 + nch
                                    t_.wait_ge(smBB, 16 * H * 2 + 16 * (c + 1))
                                    if kb >= 1:
                                        t_.wait_ge(smNM, H * 2 + kb)
                                    mm = t_.matmul(pBC2[:, :], sel2[:, :],
                                                   drow2[0:2, nch * 512:(nch + 1) * 512],
                                                   start=True, stop=True)
                                    mm.then_inc(smRC, 1)
                            t_.wait_ge(smNM, H * 2 + 32)
                        t_.wait_ge(smWT, WT_C + 16 * DC + 16 * JC + 16 * DC)
                        t_.wait_ge(smXQL, 16 * DC)
                        for nch in range(4):
                            for oc in range(DC):
                                g = nch * 8 + oc
                                if g >= 2:
                                    t_.wait_ge(smYE, g - 1)
                                for hd in range(DC):
                                    mm = t_.matmul(pY[g % 2][:, :], WOT[hd][:, oc * 128:(oc + 1) * 128],
                                                   ATT[hd][:, nch * 512:(nch + 1) * 512],
                                                   start=(hd == 0), stop=(hd == DC - 1))
                                mm.then_inc(smYM, 1)
                            for s4 in range(4):
                                nt = nch * 4 + s4
                                for oc in range(DC):
                                    kk = nt * 8 + oc
                                    t_.wait_ge(smYE, nch * 8 + oc + 1)
                                    if kk >= 2:
                                        t_.wait_ge(smYV, kk - 1)
                                    t_.transpose(ptE[kk % 2][:], yfm[oc][:, s4 * 128:(s4 + 1) * 128],
                                                 ident[:]).then_inc(smYT, 1)
                                for c2 in range(DC):
                                    kf = nt * 8 + c2
                                    if kf >= 2:
                                        t_.wait_ge(smFV, kf - 1)
                                    t_.transpose(ptF[kf % 2][:], FFN[c2][:, nt * 128:(nt + 1) * 128],
                                                 ident16[:]).then_inc(smFT, 1)

                    @blk.scalar
                    def _(s):
                        for nt in range(NT_Q):
                            s.wait_ge(smYV, 8 * (nt + 1))
                            s.activation(sqE[:], ytok[nt % 4][:], AF.Square).then_inc(smSQ, 1)
                            s.wait_ge(smVS, NTT + nt + 1)
                            asyncf(s, s.activation(stE[:, 4:5], stE[:, 2:3], AF.Ln))
                            s.activation(stE[:, 4:5], stE[:, 4:5], AF.Exp, scale=-0.5).then_inc(smRS, 1)

                    @blk.vector
                    def _(v):
                        if "ENON" not in skip:
                            v.wait_ge(smRC, 16)
                            v.reciprocal(drec[:], drec[:]).then_inc(smRC, 1)
                            for c in range(DC):
                                for nch in range(4):
                                    kb = c * 4 + nch
                                    v.wait_ge(smRC, 33 + kb + 1)
                                    v.tensor_copy(recn[:], pBC2[:, :])
                                    v.tensor_tensor(ATT[c][:, nch * 512:(nch + 1) * 512],
                                                    ATT[c][:, nch * 512:(nch + 1) * 512], recn[:],
                                                    ALU.mult).then_inc(smNM, 1)
                        for nch in range(4):
                            for oc in range(DC):
                                g = nch * 8 + oc
                                v.wait_ge(smYM, g + 1)
                                if nch >= 1:
                                    v.wait_ge(smYT, (4 * nch - 1) * 8 + oc + 1)
                                v.tensor_scalar(yfm[oc][:], pY[g % 2][:, :], vpc[:, 16 + oc:17 + oc],
                                                None, ALU.add)
                                v.tensor_tensor(yfm[oc][:], yfm[oc][:],
                                                XNQR[oc][:, nch * 512:(nch + 1) * 512],
                                                ALU.add).then_inc(smYE, 1)
                            for s4 in range(4):
                                nt = nch * 4 + s4
                                for oc in range(DC):
                                    kk = nt * 8 + oc
                                    v.wait_ge(smYT, kk + 1)
                                    v.tensor_copy(ytok[s4][:, oc * 128:(oc + 1) * 128],
                                                  ptE[kk % 2][:]).then_inc(smYV, 1)
                                v.wait_ge(smSQ, NTT + nt + 1)
                                vsync(v, v.tensor_reduce(stE[:, 0:1], ytok[s4][:], axis=AX.X, op=ALU.add))
                                v.tensor_scalar(stE[:, 0:1], stE[:, 0:1], 1.0 / D, None, ALU.mult)
                                vsync(v, v.tensor_reduce(stE[:, 1:2], sqE[:], axis=AX.X, op=ALU.add))
                                v.tensor_scalar(stE[:, 2:3], stE[:, 1:2], 1.0 / D, None, ALU.mult)
                                vsync(v, v.tensor_tensor(stE[:, 3:4], stE[:, 0:1], stE[:, 0:1], ALU.mult))
                                vsync(v, v.tensor_tensor(stE[:, 2:3], stE[:, 2:3], stE[:, 3:4], ALU.subtract))
                                v.tensor_scalar(stE[:, 2:3], stE[:, 2:3], float(EPS), None,
                                                ALU.add).then_inc(smVS, 1)
                                v.wait_ge(smRS, NTT + nt + 1)
                                v.wait_ge(smSET, 16 * 10)
                                v.tensor_scalar(yln[:], ytok[s4][:], stE[:, 0:1], None, ALU.subtract)
                                v.tensor_scalar(yln[:], yln[:], stE[:, 4:5], None, ALU.mult)
                                v.tensor_tensor(attnb[:], yln[:], gm_bc[:], ALU.mult)
                                v.tensor_tensor(attnb[:], attnb[:], bt_bc[:], ALU.add)
                                v.wait_ge(smXLp[nt % 2], 16 * (nt // 2 + 1))
                                v.tensor_tensor(attnb[:], attnb[:], xts[nt % 2][:],
                                                ALU.add).then_inc(smLN2, 1)
                                for c2 in range(DC):
                                    kf = nt * 8 + c2
                                    v.wait_ge(smFT, kf + 1)
                                    v.tensor_copy(fftok[:, c2 * 128:(c2 + 1) * 128],
                                                  ptF[kf % 2][:]).then_inc(smFV, 1)
                                if nt >= 2:
                                    v.wait_ge(smOUTp[nt % 2], 16 * (nt // 2))
                                v.tensor_tensor(outstg[nt % 2][:], attnb[:], fftok[:],
                                                ALU.add).then_inc(smOUTv, 1)

        if dbg:
            with nc.Block() as blk:
                @blk.gpsimd
                def _(g):
                    n = 0
                    for name, src in [("kscr", kscr), ("qscr", qscr), ("vscr", vscr),
                                      ("hscr", hscr), ("xnqscr", xnqscr)]:
                        if name in dbg:
                            g.dma_start(out=dbg[name], in_=src).then_inc(smDBG, 16)
                            n += 16
                    if "crow" in dbg:
                        g.dma_start(out=dbg["crow"], in_=c_row[0:1, :]).then_inc(smDBG, 16)
                        n += 16
                    if "biaspc" in dbg:
                        g.dma_start(out=dbg["biaspc"], in_=bias_pc[:]).then_inc(smDBG, 16)
                        n += 16
                    if "attn" in dbg and phases >= 3:
                        dt_ = dbg["attn"].rearrange("(c p) n -> c p n", p=128)
                        for c in range(DC):
                            g.dma_start(out=dt_[c], in_=ATT[c][:]).then_inc(smDBG, 16)
                            n += 16
                    if "ffn" in dbg and phases >= 4:
                        dt_ = dbg["ffn"].rearrange("(c p) n -> c p n", p=128)
                        for c in range(DC):
                            g.dma_start(out=dt_[c], in_=FFN[c][:]).then_inc(smDBG, 16)
                            n += 16
                    g.wait_ge(smDBG, n)

        if phases < 5 and "STUB" not in skip:
            with nc.sbuf_tensor("zstub", [128, D], F32) as zt:
                with nc.Block() as blk:
                    @blk.vector
                    def _(v):
                        v.memset(zt[:], 0.0).then_inc(smOUTv, 1)

                    @blk.sync
                    def _(sy):
                        sy.wait_ge(smOUTv, 1)
                        for t in range(NT_Q):
                            sy.dma_start(out=out_t[t], in_=zt[:]).then_inc(smOUT, 16)
                        sy.wait_ge(smOUT, 16 * NT_Q)

    return nc, dbg



def _host_inputs(inputs, core):
    b, qh = core // 2, core % 2
    x = np.asarray(inputs["L0_attn_x"])
    bq = np.asarray(inputs["L0_attn_bq"]); bk = np.asarray(inputs["L0_attn_bk"])
    bo = np.asarray(inputs["L0_attn_bo"])
    vec_pc = np.empty((128, 24), np.float32)
    for vi, arr in enumerate([bq, bk, bo]):
        vec_pc[:, vi * 8:(vi + 1) * 8] = arr.reshape(8, 128).T
    return dict(
        x_kv=np.ascontiguousarray(x[b]),
        x_q=np.ascontiguousarray(x[b, qh * NQ:(qh + 1) * NQ]),
        Wq=np.asarray(inputs["L0_attn_Wq"]), Wk=np.asarray(inputs["L0_attn_Wk"]),
        Wv=np.asarray(inputs["L0_attn_Wv"]), Wo=np.asarray(inputs["L0_attn_Wo"]),
        wg=np.asarray(inputs["L0_ffn_w_gate"]), wu=np.asarray(inputs["L0_ffn_w_up"]),
        wd=np.asarray(inputs["L0_ffn_w_down"]),
        vec_pc=vec_pc,
        alpha_bc=np.ascontiguousarray(np.broadcast_to(
            np.asarray(inputs["L0_attn_adaptive_scale"])[None, :], (128, 16))).astype(np.float32),
        bv_bc=np.ascontiguousarray(np.broadcast_to(
            np.asarray(inputs["L0_attn_bv"])[None, :], (128, D))).astype(np.float32),
        gm_bc=np.ascontiguousarray(np.broadcast_to(
            np.asarray(inputs["L0_attn_gamma"])[None, :], (128, D))).astype(np.float32),
        bt_bc=np.ascontiguousarray(np.broadcast_to(
            np.asarray(inputs["L0_attn_beta"])[None, :], (128, D))).astype(np.float32),
        clip2=np.array([[float(np.asarray(inputs["L0_ffn_clip_x"])[0]),
                         float(np.asarray(inputs["L0_ffn_clip_down"])[0])] + [0.0] * 6], np.float32),
        ident=np.eye(128, dtype=np.float32),
        sel2=np.concatenate([np.repeat([[1.0, 0.0]], 64, 0), np.repeat([[0.0, 1.0]], 64, 0)],
                            0).T.astype(np.float32).copy(),
        ident16=np.eye(128, dtype=ml_dtypes.bfloat16),
        ones_row=np.ones((1, 128), np.float32),
    )


def kernel(**inputs):
    nc, _ = build_nc(phases=5)
    in_maps = [_host_inputs(inputs, c) for c in range(8)]
    res = run_bass_kernel_spmd(nc, in_maps, core_ids=list(range(8)))
    out = np.empty((4, NKV, D), np.float32)
    for c in range(8):
        b, qh = c // 2, c % 2
        out[b, qh * NQ:(qh + 1) * NQ] = res.results[c]["out"]
    return out
